# version 14
# baseline (speedup 1.0000x reference)
import contextlib
import math

import numpy as np
import ml_dtypes

import concourse.bass as bass
import concourse.mybir as mybir
from concourse.bass_utils import run_bass_kernel_spmd

F32 = mybir.dt.float32
BF16 = mybir.dt.bfloat16
AF = mybir.ActivationFunctionType
ALU = mybir.AluOpType
AX = mybir.AxisListType

D = 1024
H = 16
DK = 64
MF = 128
T = 4096
CH = 128
TILE = 512
NT = T // TILE
STAGE = 99
PSKIP = ""
LAST_P = None
LOWER = False
NCH = TILE // CH
EPS = 1e-6
LN_EPS = 1e-5
LNM = math.log(math.sqrt(MF))
DKQ = float(DK) ** -0.25


class P:
    def __init__(self):
        self.steps = []
        self.counts = {}

    def emit(self, engine, fn, waits=(), dma=False, dsem=None):
        if engine == "sync":
            dma = True
        sem = {"sync": "dx", "vector": "ve", "scalar": "sc",
               "tensor": "te", "gpsimd": "gp"}[engine]
        if dma:
            assert dsem is not None, "DMA steps must name their semaphore"
            sem = dsem
        inc = 16 if dma else 1
        self.counts[sem] = self.counts.get(sem, 0) + inc
        ws = []
        for w in waits:
            if w is None:
                continue
            ws.append(w)
        agg = {}
        for s_, v_ in ws:
            agg[s_] = max(agg.get(s_, 0), v_)
        ws = sorted(agg.items())
        self.steps.append((engine, fn, ws, sem, inc))
        return (sem, self.counts[sem])


class G:

    def __init__(self):
        self.w = None
        self.r = []

    def wr(self):
        out = list(self.r)
        if self.w is not None:
            out.append(self.w)
        return out

    def did_w(self, evt):
        self.w = evt
        self.r = []

    def did_r(self, evt):
        self.r.append(evt)


def build_program():
    nc = bass.Bass("TRN2", target_bir_lowering=LOWER, debug=False)

    def din(name, shape, dt=F32):
        return nc.dram_tensor(name, shape, dt, kind="ExternalInput").ap()

    x_h = din("x", [T, D])
    wqt_h = din("wqt", [D, D])
    wkt_h = din("wkt", [D, D])
    wvt_h = din("wvt", [D, D])
    wo_h = din("wo", [D, D])
    wffn_h = din("wffn", [D, D])
    wmod_h = din("wmod", [D, 6 * D])
    projt_h = din("projt", [DK, MF])
    crow_h = din("crow", [8, 128])
    bqc_h = din("bqc", [128, 8])
    bkc_h = din("bkc", [128, 8])
    bvc_h = din("bvc", [128, 8])
    boc_h = din("boc", [128, 8])
    bffc_h = din("bffc", [128, 8])
    idb_h = din("idb", [128, 128], BF16)
    idf8_h = din("idf8", [8, 8], F32)
    negblk_h = din("negblk", [128, 8, 16], BF16)
    maskt_h = din("maskt", [128, 128], F32)
    ones1b_h = din("ones1b", [1, 128], BF16)
    ones1f_h = din("ones1f", [1, 128], F32)

    y_h = nc.dram_tensor("y", [T, D], F32, kind="ExternalOutput").ap()

    scr_sq = nc.dram_tensor("scr_sq", [2, 2, H * TILE], BF16, kind="Internal").ap()
    scr_rowb = nc.dram_tensor("scr_rowb", [3, D], BF16, kind="Internal").ap()
    scr_rowf = nc.dram_tensor("scr_rowf", [2, D], F32, kind="Internal").ap()

    pr = P()
    global LAST_P
    LAST_P = pr
    EM = pr.emit

    def CH(engine, fns, waits=()):
        ev = None
        for i, f in enumerate(fns):
            w = list(waits) + ([ev] if ev else [])
            ev = EM(engine, f, waits=w)
        return ev

    ctx = contextlib.ExitStack()
    sbuf = lambda name, shape, dt=F32: ctx.enter_context(nc.sbuf_tensor(name, shape, dt))
    psum = lambda name, shape, dt=F32: ctx.enter_context(nc.psum_tensor(name, shape, dt))

    with ctx:
        idb = sbuf("t_idb", [128, 128], BF16)
        idf8 = sbuf("t_idf8", [8, 8], F32)
        negblk = sbuf("t_negblk", [128, 8, 16], BF16)
        maskt = sbuf("t_maskt", [128, 128], F32)
        ones1b = sbuf("t_ones1b", [1, 128], BF16)
        ones1f = sbuf("t_ones1f", [1, 128], F32)
        projt_f = sbuf("projt_f", [DK, MF], F32)
        projtaug = sbuf("projtaug", [66, MF], BF16)
        crow = sbuf("t_crow", [8, 128], F32)
        silu_row = sbuf("silu_row", [8, 128], F32)
        silu_col = sbuf("silu_col", [128, 8], F32)
        sacol = sbuf("sacol", [128, 8], F32)
        sbcol = sbuf("sbcol", [128, 8], F32)
        ca1 = sbuf("ca1", [128, 8], F32)
        cb1c = sbuf("cb1c", [128, 8], F32)
        gac = sbuf("gac", [128, 8], F32)
        gbc = sbuf("gbc", [128, 8], F32)
        tmp8 = sbuf("tmp8", [128, 8], F32)
        tmp8b = sbuf("tmp8b", [128, 8], F32)
        biasc = sbuf("biasc", [128, 40], F32)
        qkvb = sbuf("qkvb", [128, 24], F32)
        vb_bf = sbuf("vb_bf", [128, 8], BF16)
        gabo_bf = sbuf("gabo_bf", [128, 8], BF16)
        ffnb_bf = sbuf("ffnb_bf", [128, 8], BF16)
        vbrow = sbuf("vbrow", [1, D], BF16)
        gaborow = sbuf("gaborow", [1, D], BF16)
        ffnbrow = sbuf("ffnbrow", [1, D], BF16)
        cb1row_f = sbuf("cb1row_f", [1, D], F32)
        sbrow_f = sbuf("sbrow_f", [1, D], F32)
        cb1_rep = sbuf("cb1_rep", [128, D], F32)
        sb_rep = sbuf("sb_rep", [128, D], F32)

        wq_sb = sbuf("wq_sb", [128, 8, D], BF16)
        wk_sb = sbuf("wk_sb", [128, 8, D], BF16)
        wv_sb = sbuf("wv_sb", [128, 8, D], BF16)
        wot_sb = sbuf("wot_sb", [128, 8, D], BF16)
        wdt_sb = sbuf("wdt_sb", [128, 8, D], BF16)

        xb = sbuf("xb", [128, 4, D], F32)

        pool1 = sbuf("pool1", [128, 8192], BF16)
        pv1f = pool1[:].bitcast(F32)
        stage0 = pv1f[:, 0:1024]
        stage1 = pv1f[:, 1024:2048]
        hn = pool1[:, 0:1024]
        aug = pool1[0:66, :].rearrange("p (h x) -> p h x", h=16)
        scr_c = pv1f[:, 0:1024]
        scr_d = pv1f[:, 1024:2048]
        scr_e = pv1f[:, 2048:3072]
        scr_g = pv1f[:, 3072:4096]

        pool2 = sbuf("pool2", [128, 4096], BF16)
        hnT = pool2[:].rearrange("p (k x) -> p k x", k=8)
        pv2f = pool2[:].bitcast(F32)
        gq2 = pv2f[:, 0:1024]
        gq4 = pv2f[:, 1024:2048]

        pool3 = sbuf("pool3", [128, 1024], F32)
        st_ = pool3[:]
        pv3b = pool3[:].bitcast(BF16)
        q2scr = pv3b[:, 0:1024].rearrange("p (s x) -> p s x", s=2)
        sq_f = pool3[0:16, 512:1024]
        sq_hi = pv3b[0:16, 0:512]
        sq_lo = pv3b[0:16, 512:1024]

        pool4 = sbuf("pool4", [128, 1024], BF16)
        atto = pool4[:]
        attoT = pool4[:].rearrange("p (k x) -> p k x", k=8)

        pool5 = sbuf("pool5", [128, 1024], BF16)
        xd = pool5[:]
        xdT = pool5[:].rearrange("p (k x) -> p k x", k=8)

        h2t = sbuf("h2t", [128, D], F32)
        qf = sbuf("qf", [128, H, TILE], BF16)
        kf = sbuf("kf", [128, H, TILE], BF16)
        vaug = sbuf("vaug", [128, 4, H, 66], BF16)
        ssnap = sbuf("ssnap", [128, H, 66], BF16)
        kftok = sbuf("kftok", [128, 2, 4, 128], BF16)
        am = sbuf("am", [128, 2, 4, 128], BF16)
        dentmp = sbuf("dentmp", [128, 4], F32)
        rden = sbuf("rden", [128, 4], F32)

        mv6 = sbuf("mv6", [128, 12], F32)
        mvar = sbuf("mvar", [128, 4, 2], F32)
        meanv = sbuf("meanv", [128, 4], F32)
        varv = sbuf("varv", [128, 4], F32)
        rstd = sbuf("rstd", [128, 4], F32)
        lnv = sbuf("lnv", [128, 4], F32)
        lneps = sbuf("lneps", [128, 1], F32)
        epsb = sbuf("epsb", [128, 1], F32)
        tch = sbuf("tch", [1, 4], F32)

        amax = sbuf("amax", [128, 32], F32)
        acc1 = sbuf("acc1", [128, 1], F32)
        scq = sbuf("scq", [128, 32], F32)
        rscq = sbuf("rscq", [128, 32], F32)
        wstage = sbuf("wstage", [128, D], BF16)

        ps_s1 = psum("ps_s1", [128, 8, 64])
        ps_s2 = psum("ps_s2", [128, 8, 64])
        ps_z = psum("ps_z", [128, 512])
        ps_a = psum("ps_a", [128, 512])
        ps_b = psum("ps_b", [128, 512])
        ps_c = psum("ps_c", [128, 512])
        ps_d = psum("ps_d", [128, 512])
        ps_e = psum("ps_e", [128, 512])

        sems = {n: ctx.enter_context(nc.semaphore(n))
                for n in ("dcst", "sg0", "sg1", "dx0", "dx1", "dx2", "dx3",
                          "dst0", "dst1", "dst2", "dst3", "dqh", "dql", "dqb3",
                          "dqp", "ve", "sc", "te", "gp")}

        gA, gB, gC, gD, gE = G(), G(), G(), G(), G()
        g_p1 = G()
        g_p2 = G()
        g_p3 = G()
        g_p4 = G()
        g_p5 = G()
        g_qf, g_kf = G(), G()
        g_vaug = [G(), G(), G(), G()]
        g_ssnap = G()
        g_kt = [G(), G()]
        g_am = [G(), G()]
        g_wst = G()
        g_xb = [G(), G(), G(), G()]
        g_h2 = G()
        g_sq8 = G()

        def ld(dst, src, slow=False):
            if slow:
                return lambda e: e.dma_start(out=dst, in_=src, allow_slow_non_contiguous=True)
            return lambda e: e.dma_start(out=dst, in_=src)

        e_c = None
        for dst, src in [(idb[:], idb_h), (idf8[:], idf8_h), (negblk[:], negblk_h),
                         (maskt[:], maskt_h), (ones1b[:], ones1b_h), (ones1f[:], ones1f_h),
                         (projt_f[:], projt_h), (crow[:], crow_h),
                         (biasc[:, 0:8], bqc_h), (biasc[:, 8:16], bkc_h),
                         (biasc[:, 16:24], bvc_h), (biasc[:, 24:32], boc_h),
                         (biasc[:, 32:40], bffc_h)]:
            e_c = EM("sync", ld(dst, src), dsem="dcst")
        e_consts = e_c

        def s_pta(e):
            e.tensor_scalar(projtaug[0:64, :], projt_f[:], DKQ, None, ALU.mult)
            return e.memset(projtaug[64:66, :], 1.0)
        e_pta = EM("vector", s_pta, waits=[e_consts])

        e_silu = EM("scalar", lambda e: e.activation(silu_row[:], crow[:], AF.Silu),
                    waits=[e_consts])
        e_silu_t = EM("tensor",
                      lambda e: e.matmul(ps_a[:, 0:8], silu_row[:], idf8[:], start=True, stop=True),
                      waits=[e_silu, e_consts] + gA.wr())
        gA.did_w(e_silu_t)
        e_silu_c = EM("vector", lambda e: e.tensor_copy(silu_col[:], ps_a[:, 0:8]),
                      waits=[e_silu_t])
        gA.did_r(e_silu_c)

        OBLK = [1, 0, 2, 4, 3, 5]
        qf_flat = qf[:].rearrange("p a b -> p (a b)").bitcast(F32)
        stage0 = qf_flat[:, 0:1024]
        stage1 = qf_flat[:, 1024:2048]
        wslot = [stage0, stage1]
        wguard = [G(), G()]
        first_mm = [True]
        e_wm = {}
        e_modw = EM if True else None
        gB_writer_started = False
        for i, ob in enumerate(OBLK):
            if "stg" in PSKIP:
                break
            for kc in range(8):
                sl = (i * 8 + kc) % 2
                e_dma = EM("sync", ld(wslot[sl][:], wmod_h[kc * 128:(kc + 1) * 128, ob * D:(ob + 1) * D]),
                           waits=wguard[sl].wr(), dsem=("sg0", "sg1")[sl])
                wguard[sl].did_w(e_dma)
                def s_mm(sl=sl, ob=ob, kc=kc, first=first_mm[0]):
                    def f(e):
                        r = None
                        for oc in range(8):
                            r = e.matmul(ps_b[:, ob * 8 + oc: ob * 8 + oc + 1],
                                         wslot[sl][:, oc * 128:(oc + 1) * 128],
                                         silu_col[:, kc:kc + 1],
                                         start=(first and oc == 0), stop=True,
                                         skip_group_check=True)
                        return r
                    return f
                if "mod" in PSKIP:
                    e_wm[(ob, kc)] = e_dma
                    continue
                w = [e_dma, e_silu_c]
                if first_mm[0]:
                    w += gB.wr()
                e_mm = EM("tensor", s_mm(), waits=w)
                if first_mm[0]:
                    gB.did_w(e_mm)
                first_mm[0] = False
                gB.w = e_mm
                wguard[sl].did_r(e_mm)
                e_wm[(ob, kc)] = e_mm

        def mod_ok(ob):
            return e_wm.get((ob, 7), e_consts)

        e_ca1 = EM("vector", lambda e: e.tensor_single_scalar(ca1[:], ps_b[:, 8:16], 1.0, ALU.add),
                   waits=[mod_ok(1)])
        gB.did_r(e_ca1)
        e_sa = EM("vector", lambda e: e.tensor_copy(sacol[:], ps_b[:, 0:8]),
                  waits=[mod_ok(0)])
        gB.did_r(e_sa)
        e_ga_cp = EM("vector", lambda e: e.tensor_copy(gac[:], ps_b[:, 16:24]),
                     waits=[mod_ok(2)])
        gB.did_r(e_ga_cp)
        e_ga_sgn = EM("scalar", lambda e: e.activation(tmp8[:], gac[:], AF.Sign),
                      waits=[e_ga_cp])
        e_ga = CH("vector", [
            lambda e: e.scalar_tensor_tensor(gac[:], gac[:], -1.0, gac[:], ALU.mult, ALU.max),
            lambda e: e.tensor_single_scalar(gac[:], gac[:], 0.25, ALU.max),
            lambda e: e.tensor_mul(gac[:], gac[:], tmp8[:]),
        ], waits=[e_ga_sgn])
        gB.did_r(e_ga)

        def s_cb1(e):
            e.tensor_single_scalar(cb1c[:], ps_b[:, 32:40], 1.0, ALU.add)
            return e.tensor_copy(sbcol[:], ps_b[:, 24:32])
        e_cb1 = EM("vector", s_cb1, waits=[mod_ok(4), mod_ok(3)])
        gB.did_r(e_cb1)

        e_gb_cp = EM("vector", lambda e: e.tensor_copy(gbc[:], ps_b[:, 40:48]),
                     waits=[mod_ok(5)])
        gB.did_r(e_gb_cp)
        if "sgn" in PSKIP:
            e_gb_sgn = e_gb_cp
        else:
            e_gb_sgn = EM("scalar", lambda e: e.activation(tmp8b[:], gbc[:], AF.Sign),
                          waits=[e_gb_cp])

        e_gb = CH("vector", [
            lambda e: e.scalar_tensor_tensor(gbc[:], gbc[:], -1.0, gbc[:], ALU.mult, ALU.max),
            lambda e: e.tensor_single_scalar(gbc[:], gbc[:], 0.25, ALU.max),
            lambda e: e.tensor_mul(gbc[:], gbc[:], tmp8b[:]),
        ], waits=[e_gb_sgn])
        gB.did_r(e_gb)

        first_qb = [True]
        e_wcast = {}
        for wi, (wh, wsb) in enumerate([(wqt_h, wq_sb), (wkt_h, wk_sb), (wvt_h, wv_sb)]):
            if "stg" in PSKIP:
                e_wcast[(wi, 7)] = e_consts
                continue
            for kc in range(8):
                sl = (wi * 8 + kc) % 2
                e_dma = EM("sync", ld(wslot[sl][:], wh[kc * 128:(kc + 1) * 128, :]),
                           waits=wguard[sl].wr(), dsem=("sg0", "sg1")[sl])
                wguard[sl].did_w(e_dma)
                def s_qb(sl=sl, wi=wi, kc=kc, first=first_qb[0]):
                    def f(e):
                        r = None
                        for oc in range(8):
                            r = e.matmul(ps_c[:, wi * 8 + oc: wi * 8 + oc + 1],
                                         wslot[sl][:, oc * 128:(oc + 1) * 128],
                                         sacol[:, kc:kc + 1],
                                         start=(first and oc == 0), stop=True,
                                         skip_group_check=True)
                        return r
                    return f
                if "qb" in PSKIP:
                    e_qb = e_dma
                else:
                    w = [e_dma, e_sa]
                    if first_qb[0]:
                        w += gC.wr()
                    e_qb = EM("tensor", s_qb(), waits=w)
                    if first_qb[0]:
                        gC.did_w(e_qb)
                    first_qb[0] = False
                    gC.w = e_qb
                    wguard[sl].did_r(e_qb)
                def s_cast(sl=sl, wsb=wsb, kc=kc):
                    def f(e):
                        return e.tensor_scalar(wsb[:, kc, :], wslot[sl][:], ca1[:, kc:kc + 1],
                                               None, ALU.mult)
                    return f
                if "cv" in PSKIP:
                    e_cv = e_qb
                else:
                    e_cv = EM("vector", s_cast(), waits=[e_dma, e_ca1, e_qb])
                    wguard[sl].did_r(e_cv)
                e_wcast[(wi, kc)] = e_cv
        e_qb_done = pr.steps and ("te", pr.counts["te"])

        def s_qkvb(e):
            e.tensor_add(qkvb[:, 0:8], ps_c[:, 0:8], biasc[:, 0:8])
            e.tensor_add(qkvb[:, 8:16], ps_c[:, 8:16], biasc[:, 8:16])
            return e.tensor_add(qkvb[:, 16:24], ps_c[:, 16:24], biasc[:, 16:24])
        if "qkvb" in PSKIP:
            e_qkvb0 = e_qb_done
            e_qkvb = e_qb_done
        else:
            e_qkvb0 = EM("vector", s_qkvb, waits=[e_qb_done])
            e_qkvb = EM("vector", lambda e: e.tensor_copy(vb_bf[:], qkvb[:, 16:24]),
                        waits=[e_qkvb0])
        gC.did_r(e_qkvb)

        def bounce(dst_row, src_col, dram_row, waits, readers_done):
            if "bounce" in PSKIP:
                readers_done.append(waits[0])
                return waits[0]
            e1 = EM("gpsimd", ld(dram_row.rearrange("(c p) -> p c", p=128), src_col[:], slow=True),
                    waits=waits, dma=True, dsem="dqp")
            EM("gpsimd", lambda e: e.engine_nop(), waits=[e1])
            e2 = EM("gpsimd", ld(dst_row[:], dram_row.unsqueeze(0)), waits=[e1], dma=True, dsem="dqp")
            EM("gpsimd", lambda e: e.engine_nop(), waits=[e2])
            readers_done.append(e2)
            return e2

        dummy = []
        e_vbrow = bounce(vbrow, vb_bf, scr_rowb[0, :], [e_qkvb], dummy)

        e_gabo = CH("vector", [
            lambda e: e.tensor_mul(tmp8[:], gac[:], biasc[:, 24:32]),
            lambda e: e.tensor_copy(gabo_bf[:], tmp8[:]),
        ], waits=[e_ga, e_ga_sgn])
        e_gaborow = bounce(gaborow, gabo_bf, scr_rowb[1, :], [e_gabo], dummy)

        e_ffnb = CH("vector", [
            lambda e: e.tensor_mul(tmp8b[:], gbc[:], biasc[:, 32:40]),
            lambda e: e.tensor_copy(ffnb_bf[:], tmp8b[:]),
        ], waits=[e_gb, e_gb_sgn])
        e_ffnbrow = bounce(ffnbrow, ffnb_bf, scr_rowb[2, :], [e_ffnb], dummy)

        e_cb1f = EM("vector", lambda e: e.tensor_copy(tmp8[:], cb1c[:]),
                    waits=[e_cb1, e_gaborow])
        e_cb1row = bounce(cb1row_f, tmp8, scr_rowf[0, :], [e_cb1f], dummy)
        e_sbf = EM("vector", lambda e: e.tensor_copy(tmp8b[:], sbcol[:]),
                   waits=[e_cb1, e_ffnbrow])
        e_sbrow = bounce(sbrow_f, tmp8b, scr_rowf[1, :], [e_sbf], dummy)

        if "rep" not in PSKIP or "rep1" in PSKIP:
            def s_rep1(e):
                if "rep1k" in PSKIP:
                    return e.matmul(ps_d[:, 0:128], ones1f[:], ones1f[:], start=True, stop=True)
                if "rep1s2" in PSKIP:
                    e.matmul(ps_a[:, 0:8], silu_row[:], idf8[:], start=True, stop=True)
                    return e.matmul(ps_a[:, 0:8], silu_row[:], idf8[:], start=True, stop=True)
                if "rep1s" in PSKIP:
                    return e.matmul(ps_a[:, 0:8], silu_row[:], idf8[:], start=True, stop=True)
                if "rep1t" in PSKIP:
                    return e.engine_nop()
                if "rep1a" in PSKIP:
                    return e.matmul(ps_d[:, :], ones1f[:], cb1row_f[0:1, 0:512], start=True, stop=True)
                if "rep1b" in PSKIP:
                    return e.matmul(ps_e[:, :], ones1f[:], cb1row_f[0:1, 512:1024], start=True, stop=True)
                if "rep1w" in PSKIP:
                    e.matmul(ps_d[:, :], ones1f[:], wstage[0:1, 0:512], start=True, stop=True)
                    return e.matmul(ps_e[:, :], ones1f[:], wstage[0:1, 512:1024], start=True, stop=True)
                if "rep1n" in PSKIP:
                    e.matmul(ps_d[:, 0:128], ones1f[:], cb1row_f[0:1, 0:128], start=True, stop=True)
                    return e.matmul(ps_e[:, 0:128], ones1f[:], cb1row_f[0:1, 512:640], start=True, stop=True)
                e.matmul(ps_d[:, :], ones1f[:], cb1row_f[0:1, 0:512], start=True, stop=True)
                return e.matmul(ps_e[:, :], ones1f[:], cb1row_f[0:1, 512:1024], start=True, stop=True)
            rep1_waits = [e_consts] if "rep1w" in PSKIP else [e_cb1row]
            e_rep1 = EM("tensor", s_rep1, waits=rep1_waits + gD.wr() + gE.wr())
            gD.did_w(e_rep1); gE.did_w(e_rep1)

            def s_rep1v(e):
                e.tensor_copy(cb1_rep[:, 0:512], ps_d[:, :])
                return e.tensor_copy(cb1_rep[:, 512:1024], ps_e[:, :])
            if "rep1" in PSKIP:
                e_rep1v = e_rep1
                e_rep2v = e_rep1
            else:
                e_rep1v = EM("vector", s_rep1v, waits=[e_rep1])
                gD.did_r(e_rep1v); gE.did_r(e_rep1v)

            if "rep1" not in PSKIP:
                def s_rep2(e):
                    e.matmul(ps_d[:, :], ones1f[:], sbrow_f[0:1, 0:512], start=True, stop=True)
                    return e.matmul(ps_e[:, :], ones1f[:], sbrow_f[0:1, 512:1024], start=True, stop=True)
                e_rep2 = EM("tensor", s_rep2, waits=[e_sbrow] + gD.wr() + gE.wr())
                gD.did_w(e_rep2); gE.did_w(e_rep2)

                def s_rep2v(e):
                    e.tensor_copy(sb_rep[:, 0:512], ps_d[:, :])
                    return e.tensor_copy(sb_rep[:, 512:1024], ps_e[:, :])
                e_rep2v = EM("vector", s_rep2v, waits=[e_rep2])
                gD.did_r(e_rep2v); gE.did_r(e_rep2v)

        else:
            e_rep1v = e_sbrow
            e_rep2v = e_sbrow

        def quant(src_f32, dst_bf16, col_scale, wfirst, g_dst, g_src=None):
            call_w = list(wfirst) + g_p1.wr() + g_p2.wr() + g_p3.wr() + g_dst.wr()
            per_half = []
            for hh in (0, 1):
                sl = slice(hh * 512, (hh + 1) * 512)
                sq_ = slice(hh * 16, (hh + 1) * 16)
                X = src_f32[:, sl]
                DST = dst_bf16[:, sl]
                C = scr_c[:, sl]; D = scr_d[:, sl]; E = scr_e[:, sl]; G = scr_g[:, sl]
                G2 = gq2[:, sl]; G4 = gq4[:, sl]; S = st_[:, sl]
                AM = amax[:, sq_]; SQ = scq[:, sq_]; RQ = rscq[:, sq_]
                Xb = X.rearrange("p (b j) -> p b j", j=32)
                Cb = C.rearrange("p (b j) -> p b j", j=32)
                Gb = G.rearrange("p (b j) -> p b j", j=32)
                RQb = RQ.unsqueeze(2).broadcast_to([128, 16, 32])
                SQb = SQ.unsqueeze(2).broadcast_to([128, 16, 32])
                st = []
                A = st.append
                A(("vector", lambda e, X=Xb, AM=AM: e.tensor_reduce(
                    AM, X, AX.X, ALU.max, apply_absolute_value=True), []))
                A(("vector", lambda e, AM=AM, SQ=SQ: e.tensor_scalar(
                    SQ, AM, 1.0 / 7.5, 1e-30, ALU.mult, ALU.max), [0]))
                A(("vector", lambda e, SQ=SQ, RQ=RQ: e.reciprocal(RQ, SQ), [1]))
                A(("scalar", lambda e, X=X, C=C: e.activation(
                    C, X, AF.Abs), [0]))
                A(("vector", lambda e, Cb=Cb, RQb=RQb: e.tensor_tensor(
                    Cb, Cb, RQb, ALU.mult), [2, 3]))
                A(("vector", lambda e, C=C, G2=G2: e.tensor_single_scalar(
                    G2, C, 2.0, ALU.is_ge), [4]))
                A(("vector", lambda e, C=C, G4=G4: e.tensor_single_scalar(
                    G4, C, 4.0, ALU.is_ge), [4]))
                A(("vector", lambda e, G2=G2, S=S: e.tensor_scalar(
                    S, G2, 0.125, 0.125, ALU.mult, ALU.add), [5]))
                A(("vector", lambda e, G4=G4, S=S: e.scalar_tensor_tensor(
                    S, G4, 0.25, S, ALU.mult, ALU.add), [6, 7]))
                A(("vector", lambda e, G2=G2, D=D: e.tensor_scalar(
                    D, G2, -4.0, 8.0, ALU.mult, ALU.add), [5]))
                A(("vector", lambda e, G4=G4, D=D: e.scalar_tensor_tensor(
                    D, G4, -2.0, D, ALU.mult, ALU.add), [6, 9]))
                A(("vector", lambda e, C=C, D=D: e.tensor_tensor(
                    D, D, C, ALU.mult), [4, 10]))
                A(("vector", lambda e, D=D, E=E: e.tensor_copy(E, D), [11]))
                prev = 12
                for k in (16.0, 8.0, 4.0, 2.0, 1.0):
                    A(("vector", lambda e, D=D, C=C, k=k: e.tensor_single_scalar(
                        C, D, k - 0.5, ALU.is_ge), [prev]))
                    prev = len(st) - 1
                    A(("vector", lambda e, D=D, C=C, k=k: e.scalar_tensor_tensor(
                        D, C, -k, D, ALU.mult, ALU.add), [prev]))
                    prev = len(st) - 1
                A(("vector", lambda e, E=E, D=D: e.tensor_tensor(
                    E, E, D, ALU.subtract), [12, prev]))
                A(("vector", lambda e, E=E, S=S: e.tensor_tensor(
                    E, E, S, ALU.mult), [8, len(st) - 1]))
                A(("scalar", lambda e, X=X, G=G: e.activation(
                    G, X, AF.Sign), [0]))
                A(("vector", lambda e, Gb=Gb, SQb=SQb: e.tensor_tensor(
                    Gb, Gb, SQb, ALU.mult), [1, len(st) - 1]))
                if col_scale is not None:
                    A(("vector", lambda e, G=G: e.tensor_scalar(
                        G, G, col_scale, None, ALU.mult), [len(st) - 1]))
                A(("vector", lambda e, DST=DST, E=E, G=G: e.tensor_tensor(
                    DST, E, G, ALU.mult), [len(st) - 4 if col_scale is not None
                                           else len(st) - 3, len(st) - 1]))
                per_half.append(st)
            evs = [{}, {}]
            n = len(per_half[0])
            for i in range(n):
                for hh in (0, 1):
                    eng, fn, deps = per_half[hh][i]
                    w = [evs[hh][d] for d in deps]
                    if i == 0:
                        w = w + call_w
                    evs[hh][i] = EM(eng, fn, waits=w)
            finals = [evs[0][n - 1], evs[1][n - 1]]
            for gg in (g_p1, g_p2, g_p3):
                gg.did_w(finals[0])
                gg.did_r(finals[1])
            g_dst.did_w(finals[0])
            g_dst.did_r(finals[1])
            if g_src is not None:
                g_src.did_r(finals[0])
                g_src.did_r(finals[1])
            return finals

        for oc in range(8):
            if "wo" in PSKIP or "stg" in PSKIP:
                break
            sl = oc % 2
            e_dma = EM("sync", ld(wslot[sl][:], wo_h[oc * 128:(oc + 1) * 128, :]),
                       waits=wguard[sl].wr(), dsem=("sg0", "sg1")[sl])
            wguard[sl].did_w(e_dma)
            def s_wsc(sl=sl, oc=oc):
                def f(e):
                    return e.tensor_scalar(wstage[:], wslot[sl][:], gac[:, oc:oc + 1],
                                           None, ALU.mult)
                return f
            e_sc_ = EM("vector", s_wsc(), waits=[e_dma, e_ga] + g_wst.wr())
            g_wst.did_w(e_sc_)
            wguard[sl].did_r(e_sc_)

            for half, bank, gbank in ((0, ps_d, gD), (1, ps_e, gE)):
                def s_tr(half=half, bank=bank):
                    def f(e):
                        r = None
                        for j in range(4):
                            ic = half * 4 + j
                            r = e.matmul(bank[:, j * 128:(j + 1) * 128],
                                         wstage[:, ic * 128:(ic + 1) * 128], idb[:],
                                         start=True, stop=True)
                        return r
                    return f
                e_t = EM("tensor", s_tr(), waits=[e_sc_, e_consts] + gbank.wr())
                gbank.did_w(e_t)
                g_wst.did_r(e_t)
                def s_ev(oc=oc, half=half, bank=bank):
                    def f(e):
                        return e.tensor_copy(
                            wot_sb[:, half * 4:(half + 1) * 4, oc * 128:(oc + 1) * 128],
                            bank[:, 0:512].rearrange("p (k o) -> p k o", k=4))
                    return f
                e_ev = EM("vector", s_ev(), waits=[e_t])
                gbank.did_r(e_ev)
        e_wo_done = ("ve", pr.counts["ve"])

        for oc in range(8):
            if "ffn" in PSKIP or "stg" in PSKIP:
                break
            sl = oc % 2
            e_dma = EM("sync", ld(wslot[sl][:], wffn_h[oc * 128:(oc + 1) * 128, :]),
                       waits=wguard[sl].wr(), dsem=("sg0", "sg1")[sl])
            wguard[sl].did_w(e_dma)
            e_q_l = quant(wslot[sl][:], wstage[:], gbc[:, oc:oc + 1], [e_dma, e_gb],
                          g_wst, g_src=wguard[sl])
            for half, bank, gbank in ((0, ps_d, gD), (1, ps_e, gE)):
                def s_tr2(half=half, bank=bank):
                    def f(e):
                        r = None
                        for j in range(4):
                            ic = half * 4 + j
                            r = e.matmul(bank[:, j * 128:(j + 1) * 128],
                                         wstage[:, ic * 128:(ic + 1) * 128], idb[:],
                                         start=True, stop=True)
                        return r
                    return f
                e_t = EM("tensor", s_tr2(), waits=e_q_l + gbank.wr())
                gbank.did_w(e_t)
                g_wst.did_r(e_t)
                def s_ev2(oc=oc, half=half, bank=bank):
                    def f(e):
                        return e.tensor_copy(
                            wdt_sb[:, half * 4:(half + 1) * 4, oc * 128:(oc + 1) * 128],
                            bank[:, 0:512].rearrange("p (k o) -> p k o", k=4))
                    return f
                e_ev = EM("vector", s_ev2(), waits=[e_t])
                gbank.did_r(e_ev)
        e_wd_done = ("ve", pr.counts["ve"])

        def s_vones(e):
            e.memset(lneps[:], LN_EPS)
            e.memset(epsb[:], EPS)
            e.memset(vaug[:, :, :, 64:65], 1.0)
            e.memset(vaug[:, :, :, 65:66], 0.0)
            return e.memset(ssnap[:], 0.0)
        e_vones = EM("vector", s_vones, waits=[])
        g_ssnap.did_w(e_vones)
        for gg in g_vaug:
            gg.did_w(e_vones)

        if "nan1" in PSKIP:
            CH("vector", [
                lambda e: e.memset(lneps[0:1, 0:1], 0.0),
                lambda e: e.reciprocal(lneps[0:1, 0:1], lneps[0:1, 0:1]),
            ], waits=[e_vones])
        if "nan2" in PSKIP:
            CH("scalar", [
                lambda e: e.activation(lneps[0:1, 0:1], lneps[0:1, 0:1], AF.Exp, scale=1e9),
            ], waits=[e_vones])
        if "nan3" in PSKIP:
            CH("vector", [
                lambda e: e.memset(lneps[0:1, 0:1], 0.0),
                lambda e: e.memset(lneps[1:2, 0:1], 0.0),
                lambda e: e.reciprocal(lneps[0:1, 0:1], lneps[0:1, 0:1]),
                lambda e: e.tensor_tensor(lneps[0:1, 0:1], lneps[0:1, 0:1], lneps[1:2, 0:1], ALU.mult),
            ], waits=[e_vones])

        e_store = {}
        first_s_mm = [True]
        e_kv_last = [None]
        e_tch_last = [None]

        for t in range(NT):
            e_x = {}
            for c in range(NCH):
                g = 4 * t + c
                w = [e_store.get((t - 1, c))] + g_xb[c].wr()
                e_x[c] = EM("sync", ld(xb[:, c, :], x_h[g * 128:(g + 1) * 128, :]), waits=w,
                            dsem=("dx0", "dx1", "dx2", "dx3")[c])
                g_xb[c].did_w(e_x[c])

            e_hnT_ev = {}
            for c in range(NCH):
                if "ln" in PSKIP:
                    e_hnT_ev[c] = e_x[c]
                    continue
                def s_bn(c=c):
                    def f(e):
                        e.bn_stats(mv6[:, 0:6], xb[:, c, 0:512])
                        return e.bn_stats(mv6[:, 6:12], xb[:, c, 512:1024])
                    return f
                e_b1 = CH("vector", [
                    lambda e, c=c: e.bn_stats(mv6[:, 0:6], xb[:, c, 0:512]),
                    lambda e, c=c: e.bn_stats(mv6[:, 6:12], xb[:, c, 512:1024]),
                    lambda e, c=c: e.bn_aggr(mvar[:, c, :], mv6[:]),
                ], waits=[e_x[c]])
                g_xb[c].did_r(e_b1)

                if "lnr" in PSKIP:
                    e_p2 = e_b1
                else:
                    e_p2 = CH("scalar", [
                        lambda e, c=c: e.activation(lnv[:, c:c + 1], mvar[:, c, 1:2], AF.Ln, bias=lneps[:]),
                        lambda e, c=c: e.activation(rstd[:, c:c + 1], lnv[:, c:c + 1], AF.Exp, scale=-0.5),
                    ], waits=[e_b1, e_vones])

                def s_hn(c=c):
                    def f(e):
                        return e.tensor_scalar(hn[:], xb[:, c, :], mvar[:, c, 0:1],
                                               rstd[:, c:c + 1], ALU.subtract, ALU.mult)
                    return f
                e_hnw = EM("vector", s_hn(), waits=[e_p2] + g_p1.wr())
                g_p1.did_w(e_hnw)
                g_xb[c].did_r(e_hnw)

                def s_hnt(c=c):
                    def f(e):
                        r = None
                        for half, bank in ((0, ps_a), (1, ps_b)):
                            for j in range(4):
                                kc = half * 4 + j
                                r = e.matmul(bank[:, j * 128:(j + 1) * 128],
                                             hn[:, kc * 128:(kc + 1) * 128], idb[:],
                                             start=True, stop=True)
                        return r
                    return f
                e_tr = EM("tensor", s_hnt(), waits=[e_hnw] + gA.wr() + gB.wr())
                gA.did_w(e_tr); gB.did_w(e_tr)
                g_p1.did_r(e_tr)

                def s_hne(c=c):
                    def f(e):
                        e.tensor_copy(hnT[:, 0:4, c * 128:(c + 1) * 128],
                                      ps_a[:, 0:512].rearrange("p (k x) -> p k x", k=4))
                        return e.tensor_copy(hnT[:, 4:8, c * 128:(c + 1) * 128],
                                             ps_b[:, 0:512].rearrange("p (k x) -> p k x", k=4))
                    return f
                wts = [e_tr]
                if c == 0:
                    wts += g_p2.wr()
                e_hne = EM("vector", s_hne(), waits=wts)
                if c == 0:
                    g_p2.did_w(e_hne)
                g_p2.w = e_hne
                gA.did_r(e_hne); gB.did_r(e_hne)
                e_hnT_ev[c] = e_hne
            e_hnT_all = e_hnT_ev[3]
            if STAGE < 2:
                continue

            def emit_proj_side(side, wsb, bcol0, scrsq, banks):
                e_evac = {}
                for oc in range(8):
                    bank, gbank = banks[oc % 2]
                    def s_pj(oc=oc, wsb=wsb, bank=bank):
                        def f(e):
                            r = None
                            for kc in range(8):
                                r = e.matmul(bank[:, :], wsb[:, kc, oc * 128:(oc + 1) * 128],
                                             hnT[:, kc, :], start=(kc == 0), stop=(kc == 7))
                            return r
                        return f
                    wts = [e_hnT_all, e_wcast[(side, 7)]] + gbank.wr()
                    e_pj = EM("tensor", s_pj(), waits=wts)
                    gbank.did_w(e_pj)
                    g_p2.did_r(e_pj)
                    if STAGE == 20:
                        gbank.did_r(e_pj)
                        continue

                    def s_ev(oc=oc, bcol0=bcol0, bank=bank):
                        def f(e):
                            e.tensor_scalar(aug[0:64, 2 * oc, :], bank[0:64, :],
                                            qkvb[0:64, bcol0 + oc:bcol0 + oc + 1], None, ALU.add)
                            return e.tensor_scalar(aug[0:64, 2 * oc + 1, :], bank[64:128, :],
                                                   qkvb[64:128, bcol0 + oc:bcol0 + oc + 1], None, ALU.add)
                        return f
                    wts2 = [e_pj]
                    if oc == 0:
                        wts2 += g_p1.wr()
                    e_evv = EM("vector", s_ev(), waits=wts2)
                    if oc == 0:
                        g_p1.did_w(e_evv)
                    g_p1.w = e_evv
                    gbank.did_r(e_evv)
                    e_evac[oc] = e_evv
                    if STAGE == 21:
                        continue

                    if (2100 <= STAGE < 2200 and oc >= STAGE - 2100) or (2300 <= STAGE < 2400 and oc >= 1):
                        gbank.did_r(e_pj)
                        continue
                    def s_q2(oc=oc, bcol0=bcol0, bank=bank):
                        def f(e):
                            dst = q2scr[:, oc % 2, :]
                            bias = qkvb[:, bcol0 + oc:bcol0 + oc + 1]
                            if STAGE == 2301:
                                return e.activation(dst, bank[:, :], AF.Square)
                            if STAGE == 2302:
                                return e.activation(wstage[:, 0:512], bank[:, :], AF.Square, bias=bias)
                            if STAGE == 2305:
                                return e.copy(lneps[:], lneps[:])
                            if STAGE == 2306:
                                return e.activation(wstage[:, 0:512], xb[:, 0, 0:512], AF.Square)
                            if STAGE == 2307:
                                return e.activation(wstage[:, 0:512], ps_e[0:128, 0:512], AF.Square)
                            if STAGE in (2308, 2309, 2313):
                                return e.copy(wstage[:, 0:512], bank[:, :])
                            if STAGE == 2314:
                                return e.activation(wstage[:, 0:512], bank[:, :], AF.Exp)
                            if STAGE == 2315:
                                return e.activation(wstage[:, 0:512], bank[0:128, 0:512], AF.Identity)
                            if STAGE in (2303, 2304):
                                return e.activation(wstage[:, 0:512], bank[:, :], AF.Square)
                            return e.activation(dst, bank[:, :], AF.Square, bias=bias)
                        return f
                    q2w = [e_evv] + ([] if STAGE in (2304, 2305, 2306, 2307, 2308) else g_p3.wr())
                    if "q2nw" in PSKIP:
                        q2w = [e_consts]
                    e_q2 = EM("scalar", s_q2(), waits=q2w)
                    g_p3.did_w(e_q2)
                    if STAGE not in (2304, 2305, 2306, 2307, 2308):
                        gbank.did_r(e_q2)
                    else:
                        gbank.did_r(e_pj)
                    if STAGE == 215 or 2100 <= STAGE < 2200 or 2300 <= STAGE < 2400:
                        continue

                    def s_sqmm(oc=oc):
                        def f(e):
                            return e.matmul(ps_e[0:16, 0:TILE], negblk[:, oc, :],
                                            q2scr[:, oc % 2, :], start=(oc == 0), stop=(oc == 7))
                        return f
                    w3 = [e_q2]
                    if oc == 0:
                        w3 += gE.wr()
                    e_sqm = EM("tensor", s_sqmm(), waits=w3)
                    if oc == 0:
                        gE.did_w(e_sqm)
                    gE.w = e_sqm
                    g_p3.did_r(e_sqm)
                e_sq_last = gE.w
                if STAGE == 22 or "sqf" in PSKIP:
                    return None, e_evac

                e_sqf = CH("vector", [
                    lambda e: e.tensor_single_scalar(sq_f[:], ps_e[0:16, 0:TILE], -LNM, ALU.add),
                    lambda e: e.tensor_copy(sq_hi[:], sq_f[:]),
                    lambda e: e.tensor_tensor(sq_lo[:], sq_f[:], sq_hi[:], ALU.subtract),
                ], waits=[e_sq_last] + g_p3.wr())
                g_p3.did_w(e_sqf)
                gE.did_r(e_sqf)

                def s_b1(scrsq=scrsq):
                    return lambda e: e.dma_start(out=scrsq[0].rearrange("(h x) -> h x", h=16),
                                                 in_=sq_hi[:])
                def s_b2(scrsq=scrsq):
                    return lambda e: e.dma_start(out=scrsq[1].rearrange("(h x) -> h x", h=16),
                                                 in_=sq_lo[:])
                if STAGE == 23:
                    return None, e_evac
                e_b1 = EM("gpsimd", s_b1(), waits=[e_sqf], dma=True, dsem="dqh")
                e_b2 = EM("gpsimd", s_b2(), waits=[e_sqf], dma=True, dsem="dql")
                g_p3.did_r(e_b1); g_p3.did_r(e_b2)

                def s_b3(scrsq=scrsq):
                    return lambda e: e.dma_start(out=aug[64:66, :, :],
                                                 in_=scrsq.rearrange("a (h x) -> a h x", h=16))
                if STAGE == 24:
                    return None, e_evac
                e_b3 = EM("gpsimd", s_b3(), waits=[e_b1, e_b2, g_p1.w], dma=True, dsem="dqb3")
                g_p1.w = e_b3
                return e_b3, e_evac

            def emit_u_side(phidst, g_phi, e_b3, e_evac, banks):
                e_exp = {}
                for h in range(H):
                    bank, gbank = banks[h % 2]
                    def s_u(h=h, bank=bank):
                        def f(e):
                            return e.matmul(bank[:, :], projtaug[:], aug[0:66, h, :],
                                            start=True, stop=True)
                        return f
                    wts = [e_b3, e_evac[h // 2], e_pta] + gbank.wr()
                    e_u = EM("tensor", s_u(), waits=wts)
                    gbank.did_w(e_u)
                    g_p1.did_r(e_u)

                    tw = [e_u] + ([e_tch_last[0]] if e_tch_last[0] else [])
                    def s_tch(bank=bank):
                        def f(e):
                            return e.tensor_copy(tch[0:1, 0:1], bank[0:1, 0:1])
                        return f
                    e_tch = EM("vector", s_tch(), waits=tw)
                    e_tch_last[0] = e_tch
                    gbank.did_r(e_tch)

                    def s_exp(h=h, phidst=phidst, bank=bank):
                        def f(e):
                            return e.activation(phidst[:, h, :], bank[:, :], AF.Exp)
                        return f
                    wts = [e_tch]
                    if h == 0:
                        wts += g_phi.wr()
                    e_ex = EM("scalar", s_exp(), waits=wts)
                    if h == 0:
                        g_phi.did_w(e_ex)
                    g_phi.w = e_ex
                    gbank.did_r(e_ex)
                    e_exp[h] = e_ex

                def s_eps(phidst=phidst):
                    def f(e):
                        return e.activation(
                            phidst[:].rearrange("p a b -> p (a b)"),
                            phidst[:].rearrange("p a b -> p (a b)"),
                            AF.Identity, bias=epsb[:])
                    return f
                e_eps = EM("scalar", s_eps(), waits=[e_exp[H - 1], e_vones])
                g_phi.w = e_eps
                return e_eps

            def emit_v():
                for c in range(NCH):
                    e_vmm_l = []
                    for half, bank, gbank in ((0, ps_a, gA), (1, ps_b, gB)):
                        def s_v(c=c, half=half, bank=bank):
                            def f(e):
                                for kc in range(8):
                                    e.matmul(bank[:, :], hnT[:, kc, c * 128:(c + 1) * 128],
                                             wv_sb[:, kc, half * 512:(half + 1) * 512],
                                             start=(kc == 0), stop=False)
                                return e.matmul(bank[:, :], ones1b[:],
                                                vbrow[0:1, half * 512:(half + 1) * 512],
                                                start=False, stop=True)
                            return f
                        wts = [e_hnT_ev[c], e_vbrow, e_wcast[(2, 7)]] + gbank.wr()
                        e_vmm = EM("tensor", s_v(), waits=wts)
                        gbank.did_w(e_vmm)
                        g_p2.did_r(e_vmm)
                        e_vmm_l.append(e_vmm)

                    def s_vev(c=c):
                        def f(e):
                            e.tensor_copy(vaug[:, c, 0:8, 0:64],
                                          ps_a[:, 0:512].rearrange("p (h d) -> p h d", h=8))
                            return e.tensor_copy(vaug[:, c, 8:16, 0:64],
                                                 ps_b[:, 0:512].rearrange("p (h d) -> p h d", h=8))
                        return f
                    e_vev = EM("vector", s_vev(), waits=e_vmm_l + g_vaug[c].wr())
                    g_vaug[c].did_w(e_vev)
                    gA.did_r(e_vev); gB.did_r(e_vev)

            pj_banks = ((ps_a, gA), (ps_b, gB)) if "swapbank" in PSKIP else ((ps_c, gC), (ps_d, gD))
            e_b3q, e_evq = emit_proj_side(0, wq_sb, 0, scr_sq[0], pj_banks)
            if STAGE < 3 or STAGE in (20, 21, 215, 22, 23, 24, 25) or 2100 <= STAGE < 2200 or 2300 <= STAGE < 2400:
                continue
            emit_v()
            if STAGE < 4:
                continue
            e_qf_eps = emit_u_side(qf, g_qf, e_b3q, e_evq, ((ps_c, gC), (ps_d, gD)))
            if STAGE < 5:
                continue
            e_b3k, e_evk = emit_proj_side(1, wk_sb, 8, scr_sq[1], ((ps_a, gA), (ps_b, gB)))
            e_kf_eps = emit_u_side(kf, g_kf, e_b3k, e_evk, ((ps_c, gC), (ps_d, gD)))
            if STAGE < 6:
                continue

            for c in range(NCH):
                g = 4 * t + c
                if g > 0:
                    def s_snap(e):
                        e.tensor_copy(ssnap[:, 0:8, 0:64], ps_s1[:, :, :])
                        e.tensor_copy(ssnap[:, 8:16, 0:64], ps_s2[:, :, :])
                        return e.tensor_copy(ssnap[:, :, 64:65], ps_z[:, 0:16].unsqueeze(2))
                    e_snap = EM("vector", s_snap, waits=[e_kv_last[0]] + g_ssnap.wr())
                    g_ssnap.did_w(e_snap)
                else:
                    e_snap = e_vones

                e_div = {}
                pend_div = []
                for grp in range(4):
                    hs = [4 * grp + i for i in range(4)]
                    par = grp % 2
                    abank, gab = ((ps_a, gA), (ps_c, gC))[par]
                    ndbank, gnd = ((ps_e, gE), (ps_b, gB))[par]

                    def s_amm(hs=hs, c=c, abank=abank):
                        def f(e):
                            r = None
                            for i, h in enumerate(hs):
                                r = e.matmul(abank[:, i * 128:(i + 1) * 128],
                                             kf[:, h, c * 128:(c + 1) * 128],
                                             qf[:, h, c * 128:(c + 1) * 128],
                                             start=True, stop=True)
                            return r
                        return f
                    e_amm = EM("tensor", s_amm(), waits=[e_qf_eps, e_kf_eps] + gab.wr())
                    gab.did_w(e_amm)
                    g_qf.did_r(e_amm); g_kf.did_r(e_amm)

                    def s_amask(par=par, abank=abank):
                        def f(e):
                            return e.tensor_tensor(
                                am[:, par, :, :],
                                abank[:, 0:512].rearrange("p (i x) -> p i x", i=4),
                                maskt[:].unsqueeze(1).broadcast_to([128, 4, 128]), ALU.mult)
                        return f
                    e_amk = EM("vector", s_amask(), waits=[e_amm] + g_am[par].wr())
                    g_am[par].did_w(e_amk)
                    gab.did_r(e_amk)

                    def s_ktr(hs=hs, c=c):
                        def f(e):
                            r = None
                            for i, h in enumerate(hs):
                                r = e.matmul(ps_d[:, i * 128:(i + 1) * 128],
                                             kf[:, h, c * 128:(c + 1) * 128], idb[:],
                                             start=True, stop=True)
                            return r
                        return f
                    e_ktr = EM("tensor", s_ktr(), waits=[e_kf_eps] + gD.wr())
                    gD.did_w(e_ktr)
                    g_kf.did_r(e_ktr)

                    def s_ktev(par=par):
                        def f(e):
                            return e.tensor_copy(kftok[:, par, :, :],
                                                 ps_d[:, 0:512].rearrange("p (i x) -> p i x", i=4))
                        return f
                    e_ktev = EM("vector", s_ktev(), waits=[e_ktr] + g_kt[par].wr())
                    g_kt[par].did_w(e_ktev)
                    gD.did_r(e_ktev)

                    def s_nd(hs=hs, c=c, g=g, par=par, ndbank=ndbank):
                        def f(e):
                            r = None
                            for i, h in enumerate(hs):
                                r = e.matmul(ndbank[:, i * 66:i * 66 + 65],
                                             am[:, par, i, :], vaug[:, c, h, 0:65],
                                             start=True, stop=(g == 0),
                                             skip_group_check=True)
                                if g > 0:
                                    r = e.matmul(ndbank[:, i * 66:i * 66 + 65],
                                                 qf[:, h, c * 128:(c + 1) * 128],
                                                 ssnap[:, h, 0:65],
                                                 start=False, stop=True,
                                                 skip_group_check=True)
                            return r
                        return f
                    e_nd = EM("tensor", s_nd(),
                              waits=[e_amk, e_snap, g_vaug[c].w] + gnd.wr())
                    gnd.did_w(e_nd)
                    g_am[par].did_r(e_nd)
                    g_vaug[c].did_r(e_nd)
                    g_ssnap.did_r(e_nd)
                    g_qf.did_r(e_nd)

                    def s_kv(hs=hs, c=c, par=par, first=first_s_mm[0]):
                        def f(e):
                            r = None
                            for i, h in enumerate(hs):
                                sbank = ps_s1 if h < 8 else ps_s2
                                r = e.matmul(sbank[:, h % 8, :], kftok[:, par, i, :],
                                             vaug[:, c, h, 0:64],
                                             start=(first and h in (0, 8)),
                                             stop=True, skip_group_check=True)
                                r = e.matmul(ps_z[:, h:h + 1], kftok[:, par, i, :],
                                             vaug[:, c, h, 64:65],
                                             start=(first and h == 0),
                                             stop=True, skip_group_check=True)
                            return r
                        return f
                    e_kv = EM("tensor", s_kv(), waits=[e_ktev, e_snap])
                    if grp == 3:
                        first_s_mm[0] = False
                    g_kt[par].did_r(e_kv)
                    g_vaug[c].did_r(e_kv)
                    e_kv_last[0] = e_kv

                    pend_div.append((grp, e_nd, ndbank, gnd))
                    if grp % 2 == 0:
                        continue
                    for dgrp, de_nd, dndbank, dgnd in pend_div:
                        wts = [de_nd]
                        if dgrp == 0:
                            wts += g_p4.wr()
                        else:
                            wts += [e_div[dgrp - 1]]
                        e_dv = CH("vector", [
                            lambda e, ndbank=dndbank: e.tensor_single_scalar(
                                dentmp[:],
                                ndbank[:, 0:264].rearrange("p (i x) -> p i x", i=4)[:, :, 64:65].squeeze(2),
                                EPS, ALU.add),
                            lambda e: e.reciprocal(rden[:], dentmp[:]),
                            lambda e, grp=dgrp, ndbank=dndbank: e.tensor_tensor(
                                atto[:, grp * 256:(grp + 1) * 256].rearrange("p (i d) -> p i d", i=4),
                                ndbank[:, 0:264].rearrange("p (i x) -> p i x", i=4)[:, :, 0:64],
                                rden[:].unsqueeze(2).broadcast_to([128, 4, 64]), ALU.mult),
                        ], waits=wts)
                        if dgrp == 0:
                            g_p4.did_w(e_dv)
                        g_p4.w = e_dv
                        dgnd.did_r(e_dv)
                        e_div[dgrp] = e_dv
                    pend_div = []

                if STAGE < 7:
                    continue
                def s_aot(e):
                    r = None
                    for half, bank in ((0, ps_a), (1, ps_c)):
                        for j in range(4):
                            kc = half * 4 + j
                            r = e.matmul(bank[:, j * 128:(j + 1) * 128],
                                         atto[:, kc * 128:(kc + 1) * 128], idb[:],
                                         start=True, stop=True)
                    return r
                e_aot = EM("tensor", s_aot,
                           waits=[e_div[0], e_div[1], e_div[2], e_div[3]] + gA.wr() + gC.wr())
                gA.did_w(e_aot); gC.did_w(e_aot)
                g_p4.did_r(e_aot)

                def s_aoev(e):
                    e.tensor_copy(attoT[:, 0:4, :], ps_a[:, 0:512].rearrange("p (k x) -> p k x", k=4))
                    return e.tensor_copy(attoT[:, 4:8, :], ps_c[:, 0:512].rearrange("p (k x) -> p k x", k=4))
                e_aoev = EM("vector", s_aoev, waits=[e_aot] + g_p4.wr())
                g_p4.did_w(e_aoev)
                gA.did_r(e_aoev); gC.did_r(e_aoev)

                e_wo = {}
                for half, bank, gbank in ((0, ps_d, gD), (1, ps_e, gE)):
                    def s_wo(half=half, bank=bank):
                        def f(e):
                            for kc in range(8):
                                e.matmul(bank[:, :], attoT[:, kc, :],
                                         wot_sb[:, kc, half * 512:(half + 1) * 512],
                                         start=(kc == 0), stop=False)
                            return e.matmul(bank[:, :], ones1b[:],
                                            gaborow[0:1, half * 512:(half + 1) * 512],
                                            start=False, stop=True)
                        return f
                    wts = [e_aoev, e_gaborow, e_wo_done] + gbank.wr()
                    e_w = EM("tensor", s_wo(), waits=wts)
                    gbank.did_w(e_w)
                    g_p4.did_r(e_w)
                    e_wo[half] = e_w

                def s_woepi(c=c):
                    def f(e):
                        e.tensor_tensor(xb[:, c, 0:512], xb[:, c, 0:512], ps_d[:, :], ALU.add)
                        return e.tensor_tensor(xb[:, c, 512:1024], xb[:, c, 512:1024], ps_e[:, :], ALU.add)
                    return f
                e_epi = EM("vector", s_woepi(), waits=[e_wo[0], e_wo[1]])
                gD.did_r(e_epi); gE.did_r(e_epi)
                g_xb[c].did_r(e_epi)
                g_xb[c].w = e_epi

                if STAGE < 8:
                    continue
                e2_b1 = CH("vector", [
                    lambda e, c=c: e.bn_stats(mv6[:, 0:6], xb[:, c, 0:512]),
                    lambda e, c=c: e.bn_stats(mv6[:, 6:12], xb[:, c, 512:1024]),
                    lambda e, c=c: e.bn_aggr(mvar[:, c, :], mv6[:]),
                ], waits=[e_epi])
                g_xb[c].did_r(e2_b1)
                e2_p2 = CH("scalar", [
                    lambda e, c=c: e.activation(lnv[:, c:c + 1], mvar[:, c, 1:2], AF.Ln, bias=lneps[:]),
                    lambda e, c=c: e.activation(rstd[:, c:c + 1], lnv[:, c:c + 1], AF.Exp, scale=-0.5),
                ], waits=[e2_b1, e_vones])
                e_h2a = CH("vector", [
                    lambda e, c=c: e.tensor_scalar(h2t[:], xb[:, c, :], mvar[:, c, 0:1],
                                                   rstd[:, c:c + 1], ALU.subtract, ALU.mult),
                    lambda e: e.tensor_tensor(h2t[:], h2t[:], cb1_rep[:], ALU.mult),
                ], waits=[e2_p2, e_rep1v] + g_h2.wr())
                g_h2.did_w(e_h2a)
                g_xb[c].did_r(e_h2a)

                def s2_h2b(e):
                    return e.tensor_tensor(h2t[:], h2t[:], sb_rep[:], ALU.add)
                e_h2 = EM("gpsimd", s2_h2b, waits=[e_h2a, e_rep2v])
                g_h2.w = e_h2

                if STAGE < 9:
                    continue
                e_xq_l = quant(h2t[:], xd[:], None, [e_h2], g_p5, g_src=g_h2)

                if STAGE < 10:
                    continue
                def s_xdt(e):
                    r = None
                    for half, bank in ((0, ps_a), (1, ps_c)):
                        for j in range(4):
                            kc = half * 4 + j
                            r = e.matmul(bank[:, j * 128:(j + 1) * 128],
                                         xd[:, kc * 128:(kc + 1) * 128], idb[:],
                                         start=True, stop=True)
                    return r
                e_xdt = EM("tensor", s_xdt, waits=e_xq_l + gA.wr() + gC.wr())
                gA.did_w(e_xdt); gC.did_w(e_xdt)
                g_p5.did_r(e_xdt)

                def s_xdtev(e):
                    e.tensor_copy(xdT[:, 0:4, :], ps_a[:, 0:512].rearrange("p (k x) -> p k x", k=4))
                    return e.tensor_copy(xdT[:, 4:8, :], ps_c[:, 0:512].rearrange("p (k x) -> p k x", k=4))
                e_xdte = EM("vector", s_xdtev, waits=[e_xdt] + g_p5.wr())
                g_p5.did_w(e_xdte)
                gA.did_r(e_xdte); gC.did_r(e_xdte)

                e_ffn = {}
                for half, bank, gbank in ((0, ps_d, gD), (1, ps_e, gE)):
                    def s_ffn(half=half, bank=bank):
                        def f(e):
                            for kc in range(8):
                                e.matmul(bank[:, :], xdT[:, kc, :],
                                         wdt_sb[:, kc, half * 512:(half + 1) * 512],
                                         start=(kc == 0), stop=False)
                            return e.matmul(bank[:, :], ones1b[:],
                                            ffnbrow[0:1, half * 512:(half + 1) * 512],
                                            start=False, stop=True)
                        return f
                    wts = [e_xdte, e_ffnbrow, e_wd_done] + gbank.wr()
                    e_f = EM("tensor", s_ffn(), waits=wts)
                    gbank.did_w(e_f)
                    g_p5.did_r(e_f)
                    e_ffn[half] = e_f

                def s_ffnepi(c=c):
                    def f(e):
                        e.tensor_tensor(xb[:, c, 0:512], xb[:, c, 0:512], ps_d[:, :], ALU.add)
                        return e.tensor_tensor(xb[:, c, 512:1024], xb[:, c, 512:1024], ps_e[:, :], ALU.add)
                    return f
                e_fepi = EM("vector", s_ffnepi(), waits=[e_ffn[0], e_ffn[1]])
                gD.did_r(e_fepi); gE.did_r(e_fepi)
                g_xb[c].did_r(e_fepi)
                g_xb[c].w = e_fepi

                def s_store(c=c, g=g):
                    def f(e):
                        return e.dma_start(out=y_h[g * 128:(g + 1) * 128, :], in_=xb[:, c, :])
                    return f
                e_st = EM("scalar", s_store(), waits=[e_fepi], dma=True,
                          dsem=("dst0", "dst1", "dst2", "dst3")[c])
                g_xb[c].did_r(e_st)
                e_store[(t, c)] = e_st

        final_dst = {s: pr.counts.get(s, 0) for s in ("dst0", "dst1", "dst2", "dst3")}

        with nc.Block() as block:
            for engname in ("sync", "vector", "scalar", "tensor", "gpsimd"):
                def engfn(eng, engname=engname):
                    for (en, fn, ws, sem, inc) in pr.steps:
                        if en != engname:
                            continue
                        for (wsem, wval) in ws:
                            eng.wait_ge(sems[wsem], wval)
                        inst = fn(eng)
                        inst.then_inc(sems[sem], inc)
                    if engname == "scalar":
                        for s_, v_ in final_dst.items():
                            if v_:
                                eng.wait_ge(sems[s_], v_)
                getattr(block, engname)(engfn)

    return nc


_NC_CACHE = [None]


def kernel(L0_attn_x, L0_attn_Wq, L0_attn_bq, L0_attn_Wk, L0_attn_bk,
           L0_attn_Wv, L0_attn_bv, L0_attn_Wo, L0_attn_bo, L0_attn_proj,
           L0_ffn_weight, L0_ffn_bias, L0_c, L0_wmod):
    x = np.asarray(L0_attn_x)
    assert x.shape == (8, T, D)

    def colsof(b):
        return np.ascontiguousarray(np.asarray(b, np.float32).reshape(8, 128).T)

    common = dict(
        wqt=np.ascontiguousarray(np.asarray(L0_attn_Wq, np.float32).T),
        wkt=np.ascontiguousarray(np.asarray(L0_attn_Wk, np.float32).T),
        wvt=np.ascontiguousarray(np.asarray(L0_attn_Wv, np.float32).T),
        wo=np.ascontiguousarray(np.asarray(L0_attn_Wo, np.float32)),
        wffn=np.ascontiguousarray(np.asarray(L0_ffn_weight, np.float32)),
        wmod=np.ascontiguousarray(np.asarray(L0_wmod, np.float32)),
        projt=np.ascontiguousarray(np.asarray(L0_attn_proj, np.float32).T),
        bqc=colsof(L0_attn_bq), bkc=colsof(L0_attn_bk), bvc=colsof(L0_attn_bv),
        boc=colsof(L0_attn_bo), bffc=colsof(L0_ffn_bias),
        idb=np.eye(128, dtype=ml_dtypes.bfloat16),
        idf8=np.eye(8, dtype=np.float32),
        maskt=np.triu(np.ones((128, 128), dtype=np.float32)),
        ones1b=np.ones((1, 128), dtype=ml_dtypes.bfloat16),
        ones1f=np.ones((1, 128), dtype=np.float32),
    )
    negblk = np.zeros((128, 8, 16), dtype=np.float32)
    for oc in range(8):
        negblk[0:64, oc, 2 * oc] = -1.0 / 16.0
        negblk[64:128, oc, 2 * oc + 1] = -1.0 / 16.0
    common["negblk"] = negblk.astype(ml_dtypes.bfloat16)

    cmat = np.asarray(L0_c, np.float32)
    in_maps = []
    for b in range(8):
        m = dict(common)
        m["x"] = np.ascontiguousarray(np.asarray(x[b], np.float32))
        m["crow"] = np.ascontiguousarray(cmat[b].reshape(8, 128))
        in_maps.append(m)

    nc = build_program()
    res = run_bass_kernel_spmd(nc, in_maps, core_ids=list(range(8)))
    out = np.stack([np.asarray(r["y"], np.float32) for r in res.results], axis=0)
    return out


# version 15
# speedup vs baseline: 1.0075x; 1.0075x over previous
import contextlib
import math

import numpy as np
import ml_dtypes

import concourse.bass as bass
import concourse.mybir as mybir
from concourse.bass_utils import run_bass_kernel_spmd

F32 = mybir.dt.float32
BF16 = mybir.dt.bfloat16
AF = mybir.ActivationFunctionType
ALU = mybir.AluOpType
AX = mybir.AxisListType

D = 1024
H = 16
DK = 64
MF = 128
T = 4096
CH = 128
TILE = 512
NT = T // TILE
STAGE = 99
PSKIP = ""
LAST_P = None
LOWER = False
NCH = TILE // CH
EPS = 1e-6
LN_EPS = 1e-5
LNM = math.log(math.sqrt(MF))
DKQ = float(DK) ** -0.25


class P:
    def __init__(self):
        self.steps = []
        self.counts = {}

    def emit(self, engine, fn, waits=(), dma=False, dsem=None):
        if engine == "sync":
            dma = True
        sem = {"sync": "dx", "vector": "ve", "scalar": "sc",
               "tensor": "te", "gpsimd": "gp"}[engine]
        if dma:
            assert dsem is not None, "DMA steps must name their semaphore"
            sem = dsem
        inc = 16 if dma else 1
        self.counts[sem] = self.counts.get(sem, 0) + inc
        ws = []
        for w in waits:
            if w is None:
                continue
            ws.append(w)
        agg = {}
        for s_, v_ in ws:
            agg[s_] = max(agg.get(s_, 0), v_)
        ws = sorted(agg.items())
        self.steps.append((engine, fn, ws, sem, inc))
        return (sem, self.counts[sem])


class G:

    def __init__(self):
        self.w = None
        self.r = []

    def wr(self):
        out = list(self.r)
        if self.w is not None:
            out.append(self.w)
        return out

    def did_w(self, evt):
        self.w = evt
        self.r = []

    def did_r(self, evt):
        self.r.append(evt)


def build_program():
    nc = bass.Bass("TRN2", target_bir_lowering=LOWER, debug=False)

    def din(name, shape, dt=F32):
        return nc.dram_tensor(name, shape, dt, kind="ExternalInput").ap()

    x_h = din("x", [T, D])
    wqt_h = din("wqt", [D, D])
    wkt_h = din("wkt", [D, D])
    wvt_h = din("wvt", [D, D])
    wo_h = din("wo", [D, D])
    wffn_h = din("wffn", [D, D])
    wmod_h = din("wmod", [D, 6 * D])
    projt_h = din("projt", [DK, MF])
    crow_h = din("crow", [8, 128])
    bqc_h = din("bqc", [128, 8])
    bkc_h = din("bkc", [128, 8])
    bvc_h = din("bvc", [128, 8])
    boc_h = din("boc", [128, 8])
    bffc_h = din("bffc", [128, 8])
    idb_h = din("idb", [128, 128], BF16)
    idf8_h = din("idf8", [8, 8], F32)
    negblk_h = din("negblk", [128, 8, 16], BF16)
    maskt_h = din("maskt", [128, 128], F32)
    ones1b_h = din("ones1b", [1, 128], BF16)
    ones1f_h = din("ones1f", [1, 128], F32)

    y_h = nc.dram_tensor("y", [T, D], F32, kind="ExternalOutput").ap()

    scr_sq = nc.dram_tensor("scr_sq", [2, 2, H * TILE], BF16, kind="Internal").ap()
    scr_rowb = nc.dram_tensor("scr_rowb", [3, D], BF16, kind="Internal").ap()
    scr_rowf = nc.dram_tensor("scr_rowf", [2, D], F32, kind="Internal").ap()

    pr = P()
    global LAST_P
    LAST_P = pr
    EM = pr.emit

    def CH(engine, fns, waits=()):
        ev = None
        for i, f in enumerate(fns):
            w = list(waits) + ([ev] if ev else [])
            ev = EM(engine, f, waits=w)
        return ev

    ctx = contextlib.ExitStack()
    sbuf = lambda name, shape, dt=F32: ctx.enter_context(nc.sbuf_tensor(name, shape, dt))
    psum = lambda name, shape, dt=F32: ctx.enter_context(nc.psum_tensor(name, shape, dt))

    with ctx:
        idb = sbuf("t_idb", [128, 128], BF16)
        idf8 = sbuf("t_idf8", [8, 8], F32)
        negblk = sbuf("t_negblk", [128, 8, 16], BF16)
        maskt = sbuf("t_maskt", [128, 128], F32)
        ones1b = sbuf("t_ones1b", [1, 128], BF16)
        ones1f = sbuf("t_ones1f", [1, 128], F32)
        projt_f = sbuf("projt_f", [DK, MF], F32)
        projtaug = sbuf("projtaug", [66, MF], BF16)
        crow = sbuf("t_crow", [8, 128], F32)
        silu_row = sbuf("silu_row", [8, 128], F32)
        silu_col = sbuf("silu_col", [128, 8], F32)
        sacol = sbuf("sacol", [128, 8], F32)
        sbcol = sbuf("sbcol", [128, 8], F32)
        ca1 = sbuf("ca1", [128, 8], F32)
        cb1c = sbuf("cb1c", [128, 8], F32)
        gac = sbuf("gac", [128, 8], F32)
        gbc = sbuf("gbc", [128, 8], F32)
        tmp8 = sbuf("tmp8", [128, 8], F32)
        tmp8b = sbuf("tmp8b", [128, 8], F32)
        biasc = sbuf("biasc", [128, 40], F32)
        qkvb = sbuf("qkvb", [128, 24], F32)
        vb_bf = sbuf("vb_bf", [128, 8], BF16)
        gabo_bf = sbuf("gabo_bf", [128, 8], BF16)
        ffnb_bf = sbuf("ffnb_bf", [128, 8], BF16)
        vbrow = sbuf("vbrow", [1, D], BF16)
        gaborow = sbuf("gaborow", [1, D], BF16)
        ffnbrow = sbuf("ffnbrow", [1, D], BF16)
        cb1row_f = sbuf("cb1row_f", [1, D], F32)
        sbrow_f = sbuf("sbrow_f", [1, D], F32)
        cb1_rep = sbuf("cb1_rep", [128, D], F32)
        sb_rep = sbuf("sb_rep", [128, D], F32)

        wq_sb = sbuf("wq_sb", [128, 8, D], BF16)
        wk_sb = sbuf("wk_sb", [128, 8, D], BF16)
        wv_sb = sbuf("wv_sb", [128, 8, D], BF16)
        wot_sb = sbuf("wot_sb", [128, 8, D], BF16)
        wdt_sb = sbuf("wdt_sb", [128, 8, D], BF16)

        xb = sbuf("xb", [128, 4, D], F32)

        pool1 = sbuf("pool1", [128, 8192], BF16)
        pv1f = pool1[:].bitcast(F32)
        stage0 = pv1f[:, 0:1024]
        stage1 = pv1f[:, 1024:2048]
        hn = pool1[:, 0:1024]
        aug = pool1[0:66, :].rearrange("p (h x) -> p h x", h=16)
        scr_c = pv1f[:, 0:1024]
        scr_d = pv1f[:, 1024:2048]
        scr_e = pv1f[:, 2048:3072]
        scr_g = pv1f[:, 3072:4096]

        pool2 = sbuf("pool2", [128, 4096], BF16)
        hnT = pool2[:].rearrange("p (k x) -> p k x", k=8)
        pv2f = pool2[:].bitcast(F32)
        gq2 = pv2f[:, 0:1024]
        gq4 = pv2f[:, 1024:2048]

        pool3 = sbuf("pool3", [128, 1024], F32)
        st_ = pool3[:]
        pv3b = pool3[:].bitcast(BF16)
        q2scr = pv3b[:, 0:1024].rearrange("p (s x) -> p s x", s=2)
        sq_f = pool3[0:16, 512:1024]
        sq_hi = pv3b[0:16, 0:512]
        sq_lo = pv3b[0:16, 512:1024]

        pool4 = sbuf("pool4", [128, 1024], BF16)
        atto = pool4[:]
        attoT = pool4[:].rearrange("p (k x) -> p k x", k=8)

        pool5 = sbuf("pool5", [128, 1024], BF16)
        xd = pool5[:]
        xdT = pool5[:].rearrange("p (k x) -> p k x", k=8)

        h2t = sbuf("h2t", [128, D], F32)
        qf = sbuf("qf", [128, H, TILE], BF16)
        kf = sbuf("kf", [128, H, TILE], BF16)
        vaug = sbuf("vaug", [128, 4, H, 66], BF16)
        ssnap = sbuf("ssnap", [128, H, 66], BF16)
        kftok = sbuf("kftok", [128, 2, 4, 128], BF16)
        am = sbuf("am", [128, 2, 4, 128], BF16)
        dentmp = sbuf("dentmp", [128, 4], F32)
        rden = sbuf("rden", [128, 4], F32)

        mv6 = sbuf("mv6", [128, 12], F32)
        mvar = sbuf("mvar", [128, 4, 2], F32)
        meanv = sbuf("meanv", [128, 4], F32)
        varv = sbuf("varv", [128, 4], F32)
        rstd = sbuf("rstd", [128, 4], F32)
        lnv = sbuf("lnv", [128, 4], F32)
        lneps = sbuf("lneps", [128, 1], F32)
        epsb = sbuf("epsb", [128, 1], F32)
        tch = sbuf("tch", [1, 4], F32)

        amax = sbuf("amax", [128, 32], F32)
        acc1 = sbuf("acc1", [128, 1], F32)
        scq = sbuf("scq", [128, 32], F32)
        rscq = sbuf("rscq", [128, 32], F32)
        wstage = sbuf("wstage", [128, D], BF16)

        ps_s1 = psum("ps_s1", [128, 8, 64])
        ps_s2 = psum("ps_s2", [128, 8, 64])
        ps_z = psum("ps_z", [128, 512])
        ps_a = psum("ps_a", [128, 512])
        ps_b = psum("ps_b", [128, 512])
        ps_c = psum("ps_c", [128, 512])
        ps_d = psum("ps_d", [128, 512])
        ps_e = psum("ps_e", [128, 512])

        sems = {n: ctx.enter_context(nc.semaphore(n))
                for n in ("dcst", "sg0", "sg1", "dx0", "dx1", "dx2", "dx3",
                          "dst0", "dst1", "dst2", "dst3", "dqh", "dql", "dqb3",
                          "dqp", "ve", "sc", "te", "gp")}

        gA, gB, gC, gD, gE = G(), G(), G(), G(), G()
        g_p1 = G()
        g_p2 = G()
        g_p3 = G()
        g_p4 = G()
        g_p5 = G()
        g_qf, g_kf = G(), G()
        g_vaug = [G(), G(), G(), G()]
        g_ssnap = G()
        g_kt = [G(), G()]
        g_am = [G(), G()]
        g_wst = G()
        g_xb = [G(), G(), G(), G()]
        g_h2 = G()
        g_sq8 = G()

        def ld(dst, src, slow=False):
            if slow:
                return lambda e: e.dma_start(out=dst, in_=src, allow_slow_non_contiguous=True)
            return lambda e: e.dma_start(out=dst, in_=src)

        e_c = None
        for dst, src in [(idb[:], idb_h), (idf8[:], idf8_h), (negblk[:], negblk_h),
                         (maskt[:], maskt_h), (ones1b[:], ones1b_h), (ones1f[:], ones1f_h),
                         (projt_f[:], projt_h), (crow[:], crow_h),
                         (biasc[:, 0:8], bqc_h), (biasc[:, 8:16], bkc_h),
                         (biasc[:, 16:24], bvc_h), (biasc[:, 24:32], boc_h),
                         (biasc[:, 32:40], bffc_h)]:
            e_c = EM("sync", ld(dst, src), dsem="dcst")
        e_consts = e_c

        def s_pta(e):
            e.tensor_scalar(projtaug[0:64, :], projt_f[:], DKQ, None, ALU.mult)
            return e.memset(projtaug[64:66, :], 1.0)
        e_pta = EM("vector", s_pta, waits=[e_consts])

        e_silu = EM("scalar", lambda e: e.activation(silu_row[:], crow[:], AF.Silu),
                    waits=[e_consts])
        e_silu_t = EM("tensor",
                      lambda e: e.matmul(ps_a[:, 0:8], silu_row[:], idf8[:], start=True, stop=True),
                      waits=[e_silu, e_consts] + gA.wr())
        gA.did_w(e_silu_t)
        e_silu_c = EM("vector", lambda e: e.tensor_copy(silu_col[:], ps_a[:, 0:8]),
                      waits=[e_silu_t])
        gA.did_r(e_silu_c)

        OBLK = [1, 0, 2, 4, 3, 5]
        qf_flat = qf[:].rearrange("p a b -> p (a b)").bitcast(F32)
        stage0 = qf_flat[:, 0:1024]
        stage1 = qf_flat[:, 1024:2048]
        wslot = [stage0, stage1]
        wguard = [G(), G()]
        first_mm = [True]
        e_wm = {}
        e_modw = EM if True else None
        gB_writer_started = False
        for i, ob in enumerate(OBLK):
            if "stg" in PSKIP:
                break
            for kc in range(8):
                sl = (i * 8 + kc) % 2
                e_dma = EM("sync", ld(wslot[sl][:], wmod_h[kc * 128:(kc + 1) * 128, ob * D:(ob + 1) * D]),
                           waits=wguard[sl].wr(), dsem=("sg0", "sg1")[sl])
                wguard[sl].did_w(e_dma)
                def s_mm(sl=sl, ob=ob, kc=kc, first=first_mm[0]):
                    def f(e):
                        r = None
                        for oc in range(8):
                            r = e.matmul(ps_b[:, ob * 8 + oc: ob * 8 + oc + 1],
                                         wslot[sl][:, oc * 128:(oc + 1) * 128],
                                         silu_col[:, kc:kc + 1],
                                         start=(first and oc == 0), stop=True,
                                         skip_group_check=True)
                        return r
                    return f
                if "mod" in PSKIP:
                    e_wm[(ob, kc)] = e_dma
                    continue
                w = [e_dma, e_silu_c]
                if first_mm[0]:
                    w += gB.wr()
                e_mm = EM("tensor", s_mm(), waits=w)
                if first_mm[0]:
                    gB.did_w(e_mm)
                first_mm[0] = False
                gB.w = e_mm
                wguard[sl].did_r(e_mm)
                e_wm[(ob, kc)] = e_mm

        def mod_ok(ob):
            return e_wm.get((ob, 7), e_consts)

        e_ca1 = EM("vector", lambda e: e.tensor_single_scalar(ca1[:], ps_b[:, 8:16], 1.0, ALU.add),
                   waits=[mod_ok(1)])
        gB.did_r(e_ca1)
        e_sa = EM("vector", lambda e: e.tensor_copy(sacol[:], ps_b[:, 0:8]),
                  waits=[mod_ok(0)])
        gB.did_r(e_sa)
        e_ga_cp = EM("vector", lambda e: e.tensor_copy(gac[:], ps_b[:, 16:24]),
                     waits=[mod_ok(2)])
        gB.did_r(e_ga_cp)
        e_ga_sgn = EM("scalar", lambda e: e.activation(tmp8[:], gac[:], AF.Sign),
                      waits=[e_ga_cp])
        e_ga = CH("vector", [
            lambda e: e.scalar_tensor_tensor(gac[:], gac[:], -1.0, gac[:], ALU.mult, ALU.max),
            lambda e: e.tensor_single_scalar(gac[:], gac[:], 0.25, ALU.max),
            lambda e: e.tensor_mul(gac[:], gac[:], tmp8[:]),
        ], waits=[e_ga_sgn])
        gB.did_r(e_ga)

        def s_cb1(e):
            e.tensor_single_scalar(cb1c[:], ps_b[:, 32:40], 1.0, ALU.add)
            return e.tensor_copy(sbcol[:], ps_b[:, 24:32])
        e_cb1 = EM("vector", s_cb1, waits=[mod_ok(4), mod_ok(3)])
        gB.did_r(e_cb1)

        e_gb_cp = EM("vector", lambda e: e.tensor_copy(gbc[:], ps_b[:, 40:48]),
                     waits=[mod_ok(5)])
        gB.did_r(e_gb_cp)
        if "sgn" in PSKIP:
            e_gb_sgn = e_gb_cp
        else:
            e_gb_sgn = EM("scalar", lambda e: e.activation(tmp8b[:], gbc[:], AF.Sign),
                          waits=[e_gb_cp])

        e_gb = CH("vector", [
            lambda e: e.scalar_tensor_tensor(gbc[:], gbc[:], -1.0, gbc[:], ALU.mult, ALU.max),
            lambda e: e.tensor_single_scalar(gbc[:], gbc[:], 0.25, ALU.max),
            lambda e: e.tensor_mul(gbc[:], gbc[:], tmp8b[:]),
        ], waits=[e_gb_sgn])
        gB.did_r(e_gb)

        first_qb = [True]
        e_wcast = {}
        for wi, (wh, wsb) in enumerate([(wqt_h, wq_sb), (wkt_h, wk_sb), (wvt_h, wv_sb)]):
            if "stg" in PSKIP:
                e_wcast[(wi, 7)] = e_consts
                continue
            for kc in range(8):
                sl = (wi * 8 + kc) % 2
                e_dma = EM("sync", ld(wslot[sl][:], wh[kc * 128:(kc + 1) * 128, :]),
                           waits=wguard[sl].wr(), dsem=("sg0", "sg1")[sl])
                wguard[sl].did_w(e_dma)
                def s_qb(sl=sl, wi=wi, kc=kc, first=first_qb[0]):
                    def f(e):
                        r = None
                        for oc in range(8):
                            r = e.matmul(ps_c[:, wi * 8 + oc: wi * 8 + oc + 1],
                                         wslot[sl][:, oc * 128:(oc + 1) * 128],
                                         sacol[:, kc:kc + 1],
                                         start=(first and oc == 0), stop=True,
                                         skip_group_check=True)
                        return r
                    return f
                if "qb" in PSKIP:
                    e_qb = e_dma
                else:
                    w = [e_dma, e_sa]
                    if first_qb[0]:
                        w += gC.wr()
                    e_qb = EM("tensor", s_qb(), waits=w)
                    if first_qb[0]:
                        gC.did_w(e_qb)
                    first_qb[0] = False
                    gC.w = e_qb
                    wguard[sl].did_r(e_qb)
                def s_cast(sl=sl, wsb=wsb, kc=kc):
                    def f(e):
                        return e.tensor_scalar(wsb[:, kc, :], wslot[sl][:], ca1[:, kc:kc + 1],
                                               None, ALU.mult)
                    return f
                if "cv" in PSKIP:
                    e_cv = e_qb
                else:
                    e_cv = EM("vector", s_cast(), waits=[e_dma, e_ca1, e_qb])
                    wguard[sl].did_r(e_cv)
                e_wcast[(wi, kc)] = e_cv
        e_qb_done = pr.steps and ("te", pr.counts["te"])

        def s_qkvb(e):
            e.tensor_add(qkvb[:, 0:8], ps_c[:, 0:8], biasc[:, 0:8])
            e.tensor_add(qkvb[:, 8:16], ps_c[:, 8:16], biasc[:, 8:16])
            return e.tensor_add(qkvb[:, 16:24], ps_c[:, 16:24], biasc[:, 16:24])
        if "qkvb" in PSKIP:
            e_qkvb0 = e_qb_done
            e_qkvb = e_qb_done
        else:
            e_qkvb0 = EM("vector", s_qkvb, waits=[e_qb_done])
            e_qkvb = EM("vector", lambda e: e.tensor_copy(vb_bf[:], qkvb[:, 16:24]),
                        waits=[e_qkvb0])
        gC.did_r(e_qkvb)

        def bounce(dst_row, src_col, dram_row, waits, readers_done):
            if "bounce" in PSKIP:
                readers_done.append(waits[0])
                return waits[0]
            e1 = EM("gpsimd", ld(dram_row.rearrange("(c p) -> p c", p=128), src_col[:], slow=True),
                    waits=waits, dma=True, dsem="dqp")
            EM("gpsimd", lambda e: e.engine_nop(), waits=[e1])
            e2 = EM("gpsimd", ld(dst_row[:], dram_row.unsqueeze(0)), waits=[e1], dma=True, dsem="dqp")
            EM("gpsimd", lambda e: e.engine_nop(), waits=[e2])
            readers_done.append(e2)
            return e2

        dummy = []
        e_vbrow = bounce(vbrow, vb_bf, scr_rowb[0, :], [e_qkvb], dummy)

        e_gabo = CH("vector", [
            lambda e: e.tensor_mul(tmp8[:], gac[:], biasc[:, 24:32]),
            lambda e: e.tensor_copy(gabo_bf[:], tmp8[:]),
        ], waits=[e_ga, e_ga_sgn])
        e_gaborow = bounce(gaborow, gabo_bf, scr_rowb[1, :], [e_gabo], dummy)

        e_ffnb = CH("vector", [
            lambda e: e.tensor_mul(tmp8b[:], gbc[:], biasc[:, 32:40]),
            lambda e: e.tensor_copy(ffnb_bf[:], tmp8b[:]),
        ], waits=[e_gb, e_gb_sgn])
        e_ffnbrow = bounce(ffnbrow, ffnb_bf, scr_rowb[2, :], [e_ffnb], dummy)

        e_cb1f = EM("vector", lambda e: e.tensor_copy(tmp8[:], cb1c[:]),
                    waits=[e_cb1, e_gaborow])
        e_cb1row = bounce(cb1row_f, tmp8, scr_rowf[0, :], [e_cb1f], dummy)
        e_sbf = EM("vector", lambda e: e.tensor_copy(tmp8b[:], sbcol[:]),
                   waits=[e_cb1, e_ffnbrow])
        e_sbrow = bounce(sbrow_f, tmp8b, scr_rowf[1, :], [e_sbf], dummy)

        if "rep" not in PSKIP or "rep1" in PSKIP:
            def s_rep1(e):
                if "rep1k" in PSKIP:
                    return e.matmul(ps_d[:, 0:128], ones1f[:], ones1f[:], start=True, stop=True)
                if "rep1s2" in PSKIP:
                    e.matmul(ps_a[:, 0:8], silu_row[:], idf8[:], start=True, stop=True)
                    return e.matmul(ps_a[:, 0:8], silu_row[:], idf8[:], start=True, stop=True)
                if "rep1s" in PSKIP:
                    return e.matmul(ps_a[:, 0:8], silu_row[:], idf8[:], start=True, stop=True)
                if "rep1t" in PSKIP:
                    return e.engine_nop()
                if "rep1a" in PSKIP:
                    return e.matmul(ps_d[:, :], ones1f[:], cb1row_f[0:1, 0:512], start=True, stop=True)
                if "rep1b" in PSKIP:
                    return e.matmul(ps_e[:, :], ones1f[:], cb1row_f[0:1, 512:1024], start=True, stop=True)
                if "rep1w" in PSKIP:
                    e.matmul(ps_d[:, :], ones1f[:], wstage[0:1, 0:512], start=True, stop=True)
                    return e.matmul(ps_e[:, :], ones1f[:], wstage[0:1, 512:1024], start=True, stop=True)
                if "rep1n" in PSKIP:
                    e.matmul(ps_d[:, 0:128], ones1f[:], cb1row_f[0:1, 0:128], start=True, stop=True)
                    return e.matmul(ps_e[:, 0:128], ones1f[:], cb1row_f[0:1, 512:640], start=True, stop=True)
                e.matmul(ps_d[:, :], ones1f[:], cb1row_f[0:1, 0:512], start=True, stop=True)
                return e.matmul(ps_e[:, :], ones1f[:], cb1row_f[0:1, 512:1024], start=True, stop=True)
            rep1_waits = [e_consts] if "rep1w" in PSKIP else [e_cb1row]
            e_rep1 = EM("tensor", s_rep1, waits=rep1_waits + gD.wr() + gE.wr())
            gD.did_w(e_rep1); gE.did_w(e_rep1)

            def s_rep1v(e):
                e.tensor_copy(cb1_rep[:, 0:512], ps_d[:, :])
                return e.tensor_copy(cb1_rep[:, 512:1024], ps_e[:, :])
            if "rep1" in PSKIP:
                e_rep1v = e_rep1
                e_rep2v = e_rep1
            else:
                e_rep1v = EM("vector", s_rep1v, waits=[e_rep1])
                gD.did_r(e_rep1v); gE.did_r(e_rep1v)

            if "rep1" not in PSKIP:
                def s_rep2(e):
                    e.matmul(ps_d[:, :], ones1f[:], sbrow_f[0:1, 0:512], start=True, stop=True)
                    return e.matmul(ps_e[:, :], ones1f[:], sbrow_f[0:1, 512:1024], start=True, stop=True)
                e_rep2 = EM("tensor", s_rep2, waits=[e_sbrow] + gD.wr() + gE.wr())
                gD.did_w(e_rep2); gE.did_w(e_rep2)

                def s_rep2v(e):
                    e.tensor_copy(sb_rep[:, 0:512], ps_d[:, :])
                    return e.tensor_copy(sb_rep[:, 512:1024], ps_e[:, :])
                e_rep2v = EM("vector", s_rep2v, waits=[e_rep2])
                gD.did_r(e_rep2v); gE.did_r(e_rep2v)

        else:
            e_rep1v = e_sbrow
            e_rep2v = e_sbrow

        def quant(src_f32, dst_bf16, col_scale, wfirst, g_dst, g_src=None):
            call_w = list(wfirst) + g_p1.wr() + g_p2.wr() + g_p3.wr() + g_dst.wr()
            per_half = []
            for hh in (0, 1):
                sl = slice(hh * 512, (hh + 1) * 512)
                sq_ = slice(hh * 16, (hh + 1) * 16)
                X = src_f32[:, sl]
                DST = dst_bf16[:, sl]
                C = scr_c[:, sl]; D = scr_d[:, sl]; E = scr_e[:, sl]; G = scr_g[:, sl]
                G2 = gq2[:, sl]; G4 = gq4[:, sl]; S = st_[:, sl]
                AM = amax[:, sq_]; SQ = scq[:, sq_]; RQ = rscq[:, sq_]
                Xb = X.rearrange("p (b j) -> p b j", j=32)
                Cb = C.rearrange("p (b j) -> p b j", j=32)
                Gb = G.rearrange("p (b j) -> p b j", j=32)
                RQb = RQ.unsqueeze(2).broadcast_to([128, 16, 32])
                SQb = SQ.unsqueeze(2).broadcast_to([128, 16, 32])
                st = []
                A = st.append
                A(("vector", lambda e, X=Xb, AM=AM: e.tensor_reduce(
                    AM, X, AX.X, ALU.max, apply_absolute_value=True), []))
                A(("vector", lambda e, AM=AM, SQ=SQ: e.tensor_scalar(
                    SQ, AM, 1.0 / 7.5, 1e-30, ALU.mult, ALU.max), [0]))
                A(("vector", lambda e, SQ=SQ, RQ=RQ: e.reciprocal(RQ, SQ), [1]))
                A(("scalar", lambda e, X=X, C=C: e.activation(
                    C, X, AF.Abs), [0]))
                A(("vector", lambda e, Cb=Cb, RQb=RQb: e.tensor_tensor(
                    Cb, Cb, RQb, ALU.mult), [2, 3]))
                A(("vector", lambda e, C=C, G2=G2: e.tensor_single_scalar(
                    G2, C, 2.0, ALU.is_ge), [4]))
                A(("vector", lambda e, C=C, G4=G4: e.tensor_single_scalar(
                    G4, C, 4.0, ALU.is_ge), [4]))
                A(("vector", lambda e, G2=G2, S=S: e.tensor_scalar(
                    S, G2, 0.125, 0.125, ALU.mult, ALU.add), [5]))
                A(("vector", lambda e, G4=G4, S=S: e.scalar_tensor_tensor(
                    S, G4, 0.25, S, ALU.mult, ALU.add), [6, 7]))
                A(("vector", lambda e, G2=G2, D=D: e.tensor_scalar(
                    D, G2, -4.0, 8.0, ALU.mult, ALU.add), [5]))
                A(("vector", lambda e, G4=G4, D=D: e.scalar_tensor_tensor(
                    D, G4, -2.0, D, ALU.mult, ALU.add), [6, 9]))
                A(("vector", lambda e, C=C, D=D: e.tensor_tensor(
                    D, D, C, ALU.mult), [4, 10]))
                A(("vector", lambda e, D=D, E=E: e.tensor_copy(E, D), [11]))
                prev = 12
                for k in (16.0, 8.0, 4.0, 2.0, 1.0):
                    A(("vector", lambda e, D=D, C=C, k=k: e.tensor_single_scalar(
                        C, D, k - 0.5, ALU.is_ge), [prev]))
                    prev = len(st) - 1
                    A(("vector", lambda e, D=D, C=C, k=k: e.scalar_tensor_tensor(
                        D, C, -k, D, ALU.mult, ALU.add), [prev]))
                    prev = len(st) - 1
                A(("vector", lambda e, E=E, D=D: e.tensor_tensor(
                    E, E, D, ALU.subtract), [12, prev]))
                A(("vector", lambda e, E=E, S=S: e.tensor_tensor(
                    E, E, S, ALU.mult), [8, len(st) - 1]))
                A(("scalar", lambda e, X=X, G=G: e.activation(
                    G, X, AF.Sign), [0]))
                A(("vector", lambda e, Gb=Gb, SQb=SQb: e.tensor_tensor(
                    Gb, Gb, SQb, ALU.mult), [1, len(st) - 1]))
                if col_scale is not None:
                    A(("vector", lambda e, G=G: e.tensor_scalar(
                        G, G, col_scale, None, ALU.mult), [len(st) - 1]))
                A(("vector", lambda e, DST=DST, E=E, G=G: e.tensor_tensor(
                    DST, E, G, ALU.mult), [len(st) - 4 if col_scale is not None
                                           else len(st) - 3, len(st) - 1]))
                per_half.append(st)
            evs = [{}, {}]
            n = len(per_half[0])
            for i in range(n):
                for hh in (0, 1):
                    eng, fn, deps = per_half[hh][i]
                    w = [evs[hh][d] for d in deps]
                    if i == 0:
                        w = w + call_w
                    evs[hh][i] = EM(eng, fn, waits=w)
            finals = [evs[0][n - 1], evs[1][n - 1]]
            for gg in (g_p1, g_p2, g_p3):
                gg.did_w(finals[0])
                gg.did_r(finals[1])
            g_dst.did_w(finals[0])
            g_dst.did_r(finals[1])
            if g_src is not None:
                g_src.did_r(finals[0])
                g_src.did_r(finals[1])
            return finals

        for oc in range(8):
            if "wo" in PSKIP or "stg" in PSKIP:
                break
            sl = oc % 2
            e_dma = EM("sync", ld(wslot[sl][:], wo_h[oc * 128:(oc + 1) * 128, :]),
                       waits=wguard[sl].wr(), dsem=("sg0", "sg1")[sl])
            wguard[sl].did_w(e_dma)
            def s_wsc(sl=sl, oc=oc):
                def f(e):
                    return e.tensor_scalar(wstage[:], wslot[sl][:], gac[:, oc:oc + 1],
                                           None, ALU.mult)
                return f
            e_sc_ = EM("vector", s_wsc(), waits=[e_dma, e_ga] + g_wst.wr())
            g_wst.did_w(e_sc_)
            wguard[sl].did_r(e_sc_)

            for half, bank, gbank in ((0, ps_d, gD), (1, ps_e, gE)):
                def s_tr(half=half, bank=bank):
                    def f(e):
                        r = None
                        for j in range(4):
                            ic = half * 4 + j
                            r = e.matmul(bank[:, j * 128:(j + 1) * 128],
                                         wstage[:, ic * 128:(ic + 1) * 128], idb[:],
                                         start=True, stop=True)
                        return r
                    return f
                e_t = EM("tensor", s_tr(), waits=[e_sc_, e_consts] + gbank.wr())
                gbank.did_w(e_t)
                g_wst.did_r(e_t)
                def s_ev(oc=oc, half=half, bank=bank):
                    def f(e):
                        return e.tensor_copy(
                            wot_sb[:, half * 4:(half + 1) * 4, oc * 128:(oc + 1) * 128],
                            bank[:, 0:512].rearrange("p (k o) -> p k o", k=4))
                    return f
                e_ev = EM("vector", s_ev(), waits=[e_t])
                gbank.did_r(e_ev)
        e_wo_done = ("ve", pr.counts["ve"])

        for oc in range(8):
            if "ffn" in PSKIP or "stg" in PSKIP:
                break
            sl = oc % 2
            e_dma = EM("sync", ld(wslot[sl][:], wffn_h[oc * 128:(oc + 1) * 128, :]),
                       waits=wguard[sl].wr(), dsem=("sg0", "sg1")[sl])
            wguard[sl].did_w(e_dma)
            e_q_l = quant(wslot[sl][:], wstage[:], gbc[:, oc:oc + 1], [e_dma, e_gb],
                          g_wst, g_src=wguard[sl])
            for half, bank, gbank in ((0, ps_d, gD), (1, ps_e, gE)):
                def s_tr2(half=half, bank=bank):
                    def f(e):
                        r = None
                        for j in range(4):
                            ic = half * 4 + j
                            r = e.matmul(bank[:, j * 128:(j + 1) * 128],
                                         wstage[:, ic * 128:(ic + 1) * 128], idb[:],
                                         start=True, stop=True)
                        return r
                    return f
                e_t = EM("tensor", s_tr2(), waits=e_q_l + gbank.wr())
                gbank.did_w(e_t)
                g_wst.did_r(e_t)
                def s_ev2(oc=oc, half=half, bank=bank):
                    def f(e):
                        return e.tensor_copy(
                            wdt_sb[:, half * 4:(half + 1) * 4, oc * 128:(oc + 1) * 128],
                            bank[:, 0:512].rearrange("p (k o) -> p k o", k=4))
                    return f
                e_ev = EM("vector", s_ev2(), waits=[e_t])
                gbank.did_r(e_ev)
        e_wd_done = ("ve", pr.counts["ve"])

        def s_vones(e):
            e.memset(lneps[:], LN_EPS)
            e.memset(epsb[:], EPS)
            e.memset(vaug[:, :, :, 64:65], 1.0)
            e.memset(vaug[:, :, :, 65:66], 0.0)
            return e.memset(ssnap[:], 0.0)
        e_vones = EM("vector", s_vones, waits=[])
        g_ssnap.did_w(e_vones)
        for gg in g_vaug:
            gg.did_w(e_vones)

        if "nan1" in PSKIP:
            CH("vector", [
                lambda e: e.memset(lneps[0:1, 0:1], 0.0),
                lambda e: e.reciprocal(lneps[0:1, 0:1], lneps[0:1, 0:1]),
            ], waits=[e_vones])
        if "nan2" in PSKIP:
            CH("scalar", [
                lambda e: e.activation(lneps[0:1, 0:1], lneps[0:1, 0:1], AF.Exp, scale=1e9),
            ], waits=[e_vones])
        if "nan3" in PSKIP:
            CH("vector", [
                lambda e: e.memset(lneps[0:1, 0:1], 0.0),
                lambda e: e.memset(lneps[1:2, 0:1], 0.0),
                lambda e: e.reciprocal(lneps[0:1, 0:1], lneps[0:1, 0:1]),
                lambda e: e.tensor_tensor(lneps[0:1, 0:1], lneps[0:1, 0:1], lneps[1:2, 0:1], ALU.mult),
            ], waits=[e_vones])

        e_store = {}
        first_s_mm = [True]
        e_kv_last = [None]
        e_tch_last = [None]

        for t in range(NT):
            e_x = {}
            for c in range(NCH):
                g = 4 * t + c
                w = [e_store.get((t - 1, c))] + g_xb[c].wr()
                e_x[c] = EM("sync", ld(xb[:, c, :], x_h[g * 128:(g + 1) * 128, :]), waits=w,
                            dsem=("dx0", "dx1", "dx2", "dx3")[c])
                g_xb[c].did_w(e_x[c])

            e_hnT_ev = {}
            for c in range(NCH):
                if "ln" in PSKIP:
                    e_hnT_ev[c] = e_x[c]
                    continue
                def s_bn(c=c):
                    def f(e):
                        e.bn_stats(mv6[:, 0:6], xb[:, c, 0:512])
                        return e.bn_stats(mv6[:, 6:12], xb[:, c, 512:1024])
                    return f
                e_b1 = CH("vector", [
                    lambda e, c=c: e.bn_stats(mv6[:, 0:6], xb[:, c, 0:512]),
                    lambda e, c=c: e.bn_stats(mv6[:, 6:12], xb[:, c, 512:1024]),
                    lambda e, c=c: e.bn_aggr(mvar[:, c, :], mv6[:]),
                ], waits=[e_x[c]])
                g_xb[c].did_r(e_b1)

                if "lnr" in PSKIP:
                    e_p2 = e_b1
                else:
                    e_p2 = CH("scalar", [
                        lambda e, c=c: e.activation(lnv[:, c:c + 1], mvar[:, c, 1:2], AF.Ln, bias=lneps[:]),
                        lambda e, c=c: e.activation(rstd[:, c:c + 1], lnv[:, c:c + 1], AF.Exp, scale=-0.5),
                    ], waits=[e_b1, e_vones])

                def s_hn(c=c):
                    def f(e):
                        return e.tensor_scalar(hn[:], xb[:, c, :], mvar[:, c, 0:1],
                                               rstd[:, c:c + 1], ALU.subtract, ALU.mult)
                    return f
                e_hnw = EM("vector", s_hn(), waits=[e_p2] + g_p1.wr())
                g_p1.did_w(e_hnw)
                g_xb[c].did_r(e_hnw)

                def s_hnt(c=c):
                    def f(e):
                        r = None
                        for half, bank in ((0, ps_a), (1, ps_b)):
                            for j in range(4):
                                kc = half * 4 + j
                                r = e.matmul(bank[:, j * 128:(j + 1) * 128],
                                             hn[:, kc * 128:(kc + 1) * 128], idb[:],
                                             start=True, stop=True)
                        return r
                    return f
                e_tr = EM("tensor", s_hnt(), waits=[e_hnw] + gA.wr() + gB.wr())
                gA.did_w(e_tr); gB.did_w(e_tr)
                g_p1.did_r(e_tr)

                def s_hne(c=c):
                    def f(e):
                        e.tensor_copy(hnT[:, 0:4, c * 128:(c + 1) * 128],
                                      ps_a[:, 0:512].rearrange("p (k x) -> p k x", k=4))
                        return e.tensor_copy(hnT[:, 4:8, c * 128:(c + 1) * 128],
                                             ps_b[:, 0:512].rearrange("p (k x) -> p k x", k=4))
                    return f
                wts = [e_tr]
                if c == 0:
                    wts += g_p2.wr()
                e_hne = EM("vector", s_hne(), waits=wts)
                if c == 0:
                    g_p2.did_w(e_hne)
                g_p2.w = e_hne
                gA.did_r(e_hne); gB.did_r(e_hne)
                e_hnT_ev[c] = e_hne
            e_hnT_all = e_hnT_ev[3]
            if STAGE < 2:
                continue

            def emit_proj_side(side, wsb, bcol0, scrsq, banks):
                e_evac = {}
                for oc in range(8):
                    bank, gbank = banks[oc % 2]
                    def s_pj(oc=oc, wsb=wsb, bank=bank):
                        def f(e):
                            r = None
                            for kc in range(8):
                                r = e.matmul(bank[:, :], wsb[:, kc, oc * 128:(oc + 1) * 128],
                                             hnT[:, kc, :], start=(kc == 0), stop=(kc == 7))
                            return r
                        return f
                    wts = [e_hnT_all, e_wcast[(side, 7)]] + gbank.wr()
                    e_pj = EM("tensor", s_pj(), waits=wts)
                    gbank.did_w(e_pj)
                    g_p2.did_r(e_pj)
                    if STAGE == 20:
                        gbank.did_r(e_pj)
                        continue

                    def s_ev(oc=oc, bcol0=bcol0, bank=bank):
                        def f(e):
                            e.tensor_scalar(aug[0:64, 2 * oc, :], bank[0:64, :],
                                            qkvb[0:64, bcol0 + oc:bcol0 + oc + 1], None, ALU.add)
                            return e.tensor_scalar(aug[0:64, 2 * oc + 1, :], bank[64:128, :],
                                                   qkvb[64:128, bcol0 + oc:bcol0 + oc + 1], None, ALU.add)
                        return f
                    wts2 = [e_pj]
                    if oc == 0:
                        wts2 += g_p1.wr()
                    e_evv = EM("vector", s_ev(), waits=wts2)
                    if oc == 0:
                        g_p1.did_w(e_evv)
                    g_p1.w = e_evv
                    gbank.did_r(e_evv)
                    e_evac[oc] = e_evv
                    if STAGE == 21:
                        continue

                    if (2100 <= STAGE < 2200 and oc >= STAGE - 2100) or (2300 <= STAGE < 2400 and oc >= 1):
                        gbank.did_r(e_pj)
                        continue
                    def s_q2(oc=oc, bcol0=bcol0, bank=bank):
                        def f(e):
                            dst = q2scr[:, oc % 2, :]
                            bias = qkvb[:, bcol0 + oc:bcol0 + oc + 1]
                            if STAGE == 2301:
                                return e.activation(dst, bank[:, :], AF.Square)
                            if STAGE == 2302:
                                return e.activation(wstage[:, 0:512], bank[:, :], AF.Square, bias=bias)
                            if STAGE == 2305:
                                return e.copy(lneps[:], lneps[:])
                            if STAGE == 2306:
                                return e.activation(wstage[:, 0:512], xb[:, 0, 0:512], AF.Square)
                            if STAGE == 2307:
                                return e.activation(wstage[:, 0:512], ps_e[0:128, 0:512], AF.Square)
                            if STAGE in (2308, 2309, 2313):
                                return e.copy(wstage[:, 0:512], bank[:, :])
                            if STAGE == 2314:
                                return e.activation(wstage[:, 0:512], bank[:, :], AF.Exp)
                            if STAGE == 2315:
                                return e.activation(wstage[:, 0:512], bank[0:128, 0:512], AF.Identity)
                            if STAGE in (2303, 2304):
                                return e.activation(wstage[:, 0:512], bank[:, :], AF.Square)
                            return e.activation(dst, bank[:, :], AF.Square, bias=bias)
                        return f
                    q2w = [e_evv] + ([] if STAGE in (2304, 2305, 2306, 2307, 2308) else g_p3.wr())
                    if "q2nw" in PSKIP:
                        q2w = [e_consts]
                    e_q2 = EM("scalar", s_q2(), waits=q2w)
                    g_p3.did_w(e_q2)
                    if STAGE not in (2304, 2305, 2306, 2307, 2308):
                        gbank.did_r(e_q2)
                    else:
                        gbank.did_r(e_pj)
                    if STAGE == 215 or 2100 <= STAGE < 2200 or 2300 <= STAGE < 2400:
                        continue

                    def s_sqmm(oc=oc):
                        def f(e):
                            return e.matmul(ps_e[0:16, 0:TILE], negblk[:, oc, :],
                                            q2scr[:, oc % 2, :], start=(oc == 0), stop=(oc == 7))
                        return f
                    w3 = [e_q2]
                    if oc == 0:
                        w3 += gE.wr()
                    e_sqm = EM("tensor", s_sqmm(), waits=w3)
                    if oc == 0:
                        gE.did_w(e_sqm)
                    gE.w = e_sqm
                    g_p3.did_r(e_sqm)
                e_sq_last = gE.w
                if STAGE == 22 or "sqf" in PSKIP:
                    return None, e_evac

                e_sqf = CH("vector", [
                    lambda e: e.tensor_single_scalar(sq_f[:], ps_e[0:16, 0:TILE], -LNM, ALU.add),
                    lambda e: e.tensor_copy(sq_hi[:], sq_f[:]),
                    lambda e: e.tensor_tensor(sq_lo[:], sq_f[:], sq_hi[:], ALU.subtract),
                ], waits=[e_sq_last] + g_p3.wr())
                g_p3.did_w(e_sqf)
                gE.did_r(e_sqf)

                def s_b1(scrsq=scrsq):
                    return lambda e: e.dma_start(out=scrsq[0].rearrange("(h x) -> h x", h=16),
                                                 in_=sq_hi[:])
                def s_b2(scrsq=scrsq):
                    return lambda e: e.dma_start(out=scrsq[1].rearrange("(h x) -> h x", h=16),
                                                 in_=sq_lo[:])
                if STAGE == 23:
                    return None, e_evac
                e_b1 = EM("gpsimd", s_b1(), waits=[e_sqf], dma=True, dsem="dqh")
                e_b2 = EM("gpsimd", s_b2(), waits=[e_sqf], dma=True, dsem="dql")
                g_p3.did_r(e_b1); g_p3.did_r(e_b2)

                def s_b3(scrsq=scrsq):
                    return lambda e: e.dma_start(out=aug[64:66, :, :],
                                                 in_=scrsq.rearrange("a (h x) -> a h x", h=16))
                if STAGE == 24:
                    return None, e_evac
                e_b3 = EM("gpsimd", s_b3(), waits=[e_b1, e_b2, g_p1.w], dma=True, dsem="dqb3")
                g_p1.w = e_b3
                return e_b3, e_evac

            def emit_u_side(phidst, g_phi, e_b3, e_evac, banks):
                e_exp = {}
                for h in range(H):
                    bank, gbank = banks[h % 2]
                    def s_u(h=h, bank=bank):
                        def f(e):
                            return e.matmul(bank[:, :], projtaug[:], aug[0:66, h, :],
                                            start=True, stop=True)
                        return f
                    wts = [e_b3, e_evac[h // 2], e_pta] + gbank.wr()
                    e_u = EM("tensor", s_u(), waits=wts)
                    gbank.did_w(e_u)
                    g_p1.did_r(e_u)

                    tw = [e_u] + ([e_tch_last[0]] if e_tch_last[0] else [])
                    def s_tch(bank=bank):
                        def f(e):
                            return e.tensor_copy(tch[0:1, 0:1], bank[0:1, 0:1])
                        return f
                    e_tch = EM("vector", s_tch(), waits=tw)
                    e_tch_last[0] = e_tch
                    gbank.did_r(e_tch)

                    def s_exp(h=h, phidst=phidst, bank=bank):
                        def f(e):
                            return e.activation(phidst[:, h, :], bank[:, :], AF.Exp)
                        return f
                    wts = [e_tch]
                    if h == 0:
                        wts += g_phi.wr()
                    e_ex = EM("scalar", s_exp(), waits=wts)
                    if h == 0:
                        g_phi.did_w(e_ex)
                    g_phi.w = e_ex
                    gbank.did_r(e_ex)
                    e_exp[h] = e_ex

                def s_eps(phidst=phidst):
                    def f(e):
                        return e.activation(
                            phidst[:].rearrange("p a b -> p (a b)"),
                            phidst[:].rearrange("p a b -> p (a b)"),
                            AF.Identity, bias=epsb[:])
                    return f
                e_eps = EM("scalar", s_eps(), waits=[e_exp[H - 1], e_vones])
                g_phi.w = e_eps
                return e_eps

            def emit_v():
                for c in range(NCH):
                    e_vmm_l = []
                    for half, bank, gbank in ((0, ps_a, gA), (1, ps_b, gB)):
                        def s_v(c=c, half=half, bank=bank):
                            def f(e):
                                for kc in range(8):
                                    e.matmul(bank[:, :], hnT[:, kc, c * 128:(c + 1) * 128],
                                             wv_sb[:, kc, half * 512:(half + 1) * 512],
                                             start=(kc == 0), stop=False)
                                return e.matmul(bank[:, :], ones1b[:],
                                                vbrow[0:1, half * 512:(half + 1) * 512],
                                                start=False, stop=True)
                            return f
                        wts = [e_hnT_ev[c], e_vbrow, e_wcast[(2, 7)]] + gbank.wr()
                        e_vmm = EM("tensor", s_v(), waits=wts)
                        gbank.did_w(e_vmm)
                        g_p2.did_r(e_vmm)
                        e_vmm_l.append(e_vmm)

                    def s_vev(c=c):
                        def f(e):
                            e.tensor_copy(vaug[:, c, 0:8, 0:64],
                                          ps_a[:, 0:512].rearrange("p (h d) -> p h d", h=8))
                            return e.tensor_copy(vaug[:, c, 8:16, 0:64],
                                                 ps_b[:, 0:512].rearrange("p (h d) -> p h d", h=8))
                        return f
                    e_vev = EM("vector", s_vev(), waits=e_vmm_l + g_vaug[c].wr())
                    g_vaug[c].did_w(e_vev)
                    gA.did_r(e_vev); gB.did_r(e_vev)

            pj_banks = ((ps_a, gA), (ps_b, gB)) if "swapbank" in PSKIP else ((ps_c, gC), (ps_d, gD))
            e_b3q, e_evq = emit_proj_side(0, wq_sb, 0, scr_sq[0], pj_banks)
            if STAGE < 3 or STAGE in (20, 21, 215, 22, 23, 24, 25) or 2100 <= STAGE < 2200 or 2300 <= STAGE < 2400:
                continue
            emit_v()
            if STAGE < 4:
                continue
            e_qf_eps = emit_u_side(qf, g_qf, e_b3q, e_evq, ((ps_c, gC), (ps_d, gD)))
            if STAGE < 5:
                continue
            e_b3k, e_evk = emit_proj_side(1, wk_sb, 8, scr_sq[1], ((ps_a, gA), (ps_b, gB)))
            e_kf_eps = emit_u_side(kf, g_kf, e_b3k, e_evk, ((ps_c, gC), (ps_d, gD)))
            if STAGE < 6:
                continue

            for c in range(NCH):
                g = 4 * t + c
                if g > 0:
                    def s_snap(e):
                        e.tensor_copy(ssnap[:, 0:8, 0:64], ps_s1[:, :, :])
                        e.tensor_copy(ssnap[:, 8:16, 0:64], ps_s2[:, :, :])
                        return e.tensor_copy(ssnap[:, :, 64:65], ps_z[:, 0:16].unsqueeze(2))
                    e_snap = EM("vector", s_snap, waits=[e_kv_last[0]] + g_ssnap.wr())
                    g_ssnap.did_w(e_snap)
                else:
                    e_snap = e_vones

                e_div = {}
                for grp in range(4):
                    hs = [4 * grp + i for i in range(4)]
                    par = grp % 2
                    abank, gab = ((ps_a, gA), (ps_c, gC))[par]
                    ndbank, gnd = ((ps_e, gE), (ps_b, gB))[par]

                    def s_amm(hs=hs, c=c, abank=abank):
                        def f(e):
                            r = None
                            for i, h in enumerate(hs):
                                r = e.matmul(abank[:, i * 128:(i + 1) * 128],
                                             kf[:, h, c * 128:(c + 1) * 128],
                                             qf[:, h, c * 128:(c + 1) * 128],
                                             start=True, stop=True)
                            return r
                        return f
                    e_amm = EM("tensor", s_amm(), waits=[e_qf_eps, e_kf_eps] + gab.wr())
                    gab.did_w(e_amm)
                    g_qf.did_r(e_amm); g_kf.did_r(e_amm)

                    def s_amask(par=par, abank=abank):
                        def f(e):
                            return e.tensor_tensor(
                                am[:, par, :, :],
                                abank[:, 0:512].rearrange("p (i x) -> p i x", i=4),
                                maskt[:].unsqueeze(1).broadcast_to([128, 4, 128]), ALU.mult)
                        return f
                    e_amk = EM("vector", s_amask(), waits=[e_amm] + g_am[par].wr())
                    g_am[par].did_w(e_amk)
                    gab.did_r(e_amk)

                    def s_ktr(hs=hs, c=c):
                        def f(e):
                            r = None
                            for i, h in enumerate(hs):
                                r = e.matmul(ps_d[:, i * 128:(i + 1) * 128],
                                             kf[:, h, c * 128:(c + 1) * 128], idb[:],
                                             start=True, stop=True)
                            return r
                        return f
                    e_ktr = EM("tensor", s_ktr(), waits=[e_kf_eps] + gD.wr())
                    gD.did_w(e_ktr)
                    g_kf.did_r(e_ktr)

                    def s_ktev(par=par):
                        def f(e):
                            return e.tensor_copy(kftok[:, par, :, :],
                                                 ps_d[:, 0:512].rearrange("p (i x) -> p i x", i=4))
                        return f
                    e_ktev = EM("vector", s_ktev(), waits=[e_ktr] + g_kt[par].wr())
                    g_kt[par].did_w(e_ktev)
                    gD.did_r(e_ktev)

                    def s_nd(hs=hs, c=c, g=g, par=par, ndbank=ndbank):
                        def f(e):
                            r = None
                            for i, h in enumerate(hs):
                                r = e.matmul(ndbank[:, i * 66:i * 66 + 65],
                                             am[:, par, i, :], vaug[:, c, h, 0:65],
                                             start=True, stop=(g == 0),
                                             skip_group_check=True)
                                if g > 0:
                                    r = e.matmul(ndbank[:, i * 66:i * 66 + 65],
                                                 qf[:, h, c * 128:(c + 1) * 128],
                                                 ssnap[:, h, 0:65],
                                                 start=False, stop=True,
                                                 skip_group_check=True)
                            return r
                        return f
                    e_nd = EM("tensor", s_nd(),
                              waits=[e_amk, e_snap, g_vaug[c].w] + gnd.wr())
                    gnd.did_w(e_nd)
                    g_am[par].did_r(e_nd)
                    g_vaug[c].did_r(e_nd)
                    g_ssnap.did_r(e_nd)
                    g_qf.did_r(e_nd)

                    def s_kv(hs=hs, c=c, par=par, first=first_s_mm[0]):
                        def f(e):
                            r = None
                            for i, h in enumerate(hs):
                                sbank = ps_s1 if h < 8 else ps_s2
                                r = e.matmul(sbank[:, h % 8, :], kftok[:, par, i, :],
                                             vaug[:, c, h, 0:64],
                                             start=(first and h in (0, 8)),
                                             stop=True, skip_group_check=True)
                                r = e.matmul(ps_z[:, h:h + 1], kftok[:, par, i, :],
                                             vaug[:, c, h, 64:65],
                                             start=(first and h == 0),
                                             stop=True, skip_group_check=True)
                            return r
                        return f
                    e_kv = EM("tensor", s_kv(), waits=[e_ktev, e_snap])
                    if grp == 3:
                        first_s_mm[0] = False
                    g_kt[par].did_r(e_kv)
                    g_vaug[c].did_r(e_kv)
                    e_kv_last[0] = e_kv

                    wts = [e_nd]
                    if grp == 0:
                        wts += g_p4.wr()
                    e_dv = CH("vector", [
                        lambda e, ndbank=ndbank: e.tensor_single_scalar(
                            dentmp[:],
                            ndbank[:, 0:264].rearrange("p (i x) -> p i x", i=4)[:, :, 64:65].squeeze(2),
                            EPS, ALU.add),
                        lambda e: e.reciprocal(rden[:], dentmp[:]),
                        lambda e, grp=grp, ndbank=ndbank: e.tensor_tensor(
                            atto[:, grp * 256:(grp + 1) * 256].rearrange("p (i d) -> p i d", i=4),
                            ndbank[:, 0:264].rearrange("p (i x) -> p i x", i=4)[:, :, 0:64],
                            rden[:].unsqueeze(2).broadcast_to([128, 4, 64]), ALU.mult),
                    ], waits=wts)
                    if grp == 0:
                        g_p4.did_w(e_dv)
                    g_p4.w = e_dv
                    gnd.did_r(e_dv)
                    e_div[grp] = e_dv

                if STAGE < 7:
                    continue
                def s_aot(e):
                    r = None
                    for half, bank in ((0, ps_a), (1, ps_c)):
                        for j in range(4):
                            kc = half * 4 + j
                            r = e.matmul(bank[:, j * 128:(j + 1) * 128],
                                         atto[:, kc * 128:(kc + 1) * 128], idb[:],
                                         start=True, stop=True)
                    return r
                e_aot = EM("tensor", s_aot,
                           waits=[e_div[0], e_div[1], e_div[2], e_div[3]] + gA.wr() + gC.wr())
                gA.did_w(e_aot); gC.did_w(e_aot)
                g_p4.did_r(e_aot)

                def s_aoev(e):
                    e.tensor_copy(attoT[:, 0:4, :], ps_a[:, 0:512].rearrange("p (k x) -> p k x", k=4))
                    return e.tensor_copy(attoT[:, 4:8, :], ps_c[:, 0:512].rearrange("p (k x) -> p k x", k=4))
                e_aoev = EM("vector", s_aoev, waits=[e_aot] + g_p4.wr())
                g_p4.did_w(e_aoev)
                gA.did_r(e_aoev); gC.did_r(e_aoev)

                e_wo = {}
                for half, bank, gbank in ((0, ps_d, gD), (1, ps_e, gE)):
                    def s_wo(half=half, bank=bank):
                        def f(e):
                            for kc in range(8):
                                e.matmul(bank[:, :], attoT[:, kc, :],
                                         wot_sb[:, kc, half * 512:(half + 1) * 512],
                                         start=(kc == 0), stop=False)
                            return e.matmul(bank[:, :], ones1b[:],
                                            gaborow[0:1, half * 512:(half + 1) * 512],
                                            start=False, stop=True)
                        return f
                    wts = [e_aoev, e_gaborow, e_wo_done] + gbank.wr()
                    e_w = EM("tensor", s_wo(), waits=wts)
                    gbank.did_w(e_w)
                    g_p4.did_r(e_w)
                    e_wo[half] = e_w

                def s_woepi(c=c):
                    def f(e):
                        e.tensor_tensor(xb[:, c, 0:512], xb[:, c, 0:512], ps_d[:, :], ALU.add)
                        return e.tensor_tensor(xb[:, c, 512:1024], xb[:, c, 512:1024], ps_e[:, :], ALU.add)
                    return f
                e_epi = EM("vector", s_woepi(), waits=[e_wo[0], e_wo[1]])
                gD.did_r(e_epi); gE.did_r(e_epi)
                g_xb[c].did_r(e_epi)
                g_xb[c].w = e_epi

                if STAGE < 8:
                    continue
                e2_b1 = CH("vector", [
                    lambda e, c=c: e.bn_stats(mv6[:, 0:6], xb[:, c, 0:512]),
                    lambda e, c=c: e.bn_stats(mv6[:, 6:12], xb[:, c, 512:1024]),
                    lambda e, c=c: e.bn_aggr(mvar[:, c, :], mv6[:]),
                ], waits=[e_epi])
                g_xb[c].did_r(e2_b1)
                e2_p2 = CH("scalar", [
                    lambda e, c=c: e.activation(lnv[:, c:c + 1], mvar[:, c, 1:2], AF.Ln, bias=lneps[:]),
                    lambda e, c=c: e.activation(rstd[:, c:c + 1], lnv[:, c:c + 1], AF.Exp, scale=-0.5),
                ], waits=[e2_b1, e_vones])
                e_h2a = CH("vector", [
                    lambda e, c=c: e.tensor_scalar(h2t[:], xb[:, c, :], mvar[:, c, 0:1],
                                                   rstd[:, c:c + 1], ALU.subtract, ALU.mult),
                    lambda e: e.tensor_tensor(h2t[:], h2t[:], cb1_rep[:], ALU.mult),
                ], waits=[e2_p2, e_rep1v] + g_h2.wr())
                g_h2.did_w(e_h2a)
                g_xb[c].did_r(e_h2a)

                def s2_h2b(e):
                    return e.tensor_tensor(h2t[:], h2t[:], sb_rep[:], ALU.add)
                e_h2 = EM("gpsimd", s2_h2b, waits=[e_h2a, e_rep2v])
                g_h2.w = e_h2

                if STAGE < 9:
                    continue
                e_xq_l = quant(h2t[:], xd[:], None, [e_h2], g_p5, g_src=g_h2)

                if STAGE < 10:
                    continue
                def s_xdt(e):
                    r = None
                    for half, bank in ((0, ps_a), (1, ps_c)):
                        for j in range(4):
                            kc = half * 4 + j
                            r = e.matmul(bank[:, j * 128:(j + 1) * 128],
                                         xd[:, kc * 128:(kc + 1) * 128], idb[:],
                                         start=True, stop=True)
                    return r
                e_xdt = EM("tensor", s_xdt, waits=e_xq_l + gA.wr() + gC.wr())
                gA.did_w(e_xdt); gC.did_w(e_xdt)
                g_p5.did_r(e_xdt)

                def s_xdtev(e):
                    e.tensor_copy(xdT[:, 0:4, :], ps_a[:, 0:512].rearrange("p (k x) -> p k x", k=4))
                    return e.tensor_copy(xdT[:, 4:8, :], ps_c[:, 0:512].rearrange("p (k x) -> p k x", k=4))
                e_xdte = EM("vector", s_xdtev, waits=[e_xdt] + g_p5.wr())
                g_p5.did_w(e_xdte)
                gA.did_r(e_xdte); gC.did_r(e_xdte)

                e_ffn = {}
                for half, bank, gbank in ((0, ps_d, gD), (1, ps_e, gE)):
                    def s_ffn(half=half, bank=bank):
                        def f(e):
                            for kc in range(8):
                                e.matmul(bank[:, :], xdT[:, kc, :],
                                         wdt_sb[:, kc, half * 512:(half + 1) * 512],
                                         start=(kc == 0), stop=False)
                            return e.matmul(bank[:, :], ones1b[:],
                                            ffnbrow[0:1, half * 512:(half + 1) * 512],
                                            start=False, stop=True)
                        return f
                    wts = [e_xdte, e_ffnbrow, e_wd_done] + gbank.wr()
                    e_f = EM("tensor", s_ffn(), waits=wts)
                    gbank.did_w(e_f)
                    g_p5.did_r(e_f)
                    e_ffn[half] = e_f

                e_st_l = []
                for half, bank, gbank in ((0, ps_d, gD), (1, ps_e, gE)):
                    def s_fep(c=c, half=half, bank=bank):
                        def f(e):
                            return e.tensor_tensor(
                                xb[:, c, half * 512:(half + 1) * 512],
                                xb[:, c, half * 512:(half + 1) * 512],
                                bank[:, :], ALU.add)
                        return f
                    e_fepi = EM("vector", s_fep(), waits=[e_ffn[half]])
                    gbank.did_r(e_fepi)
                    g_xb[c].did_r(e_fepi)
                    g_xb[c].w = e_fepi

                    def s_store(c=c, g=g, half=half):
                        def f(e):
                            return e.dma_start(
                                out=y_h[g * 128:(g + 1) * 128, half * 512:(half + 1) * 512],
                                in_=xb[:, c, half * 512:(half + 1) * 512])
                        return f
                    e_st = EM("scalar", s_store(), waits=[e_fepi], dma=True,
                              dsem=("dst0", "dst1", "dst2", "dst3")[c])
                    g_xb[c].did_r(e_st)
                    e_st_l.append(e_st)
                e_store[(t, c)] = e_st_l[1]

        final_dst = {s: pr.counts.get(s, 0) for s in ("dst0", "dst1", "dst2", "dst3")}

        with nc.Block() as block:
            for engname in ("sync", "vector", "scalar", "tensor", "gpsimd"):
                def engfn(eng, engname=engname):
                    for (en, fn, ws, sem, inc) in pr.steps:
                        if en != engname:
                            continue
                        for (wsem, wval) in ws:
                            eng.wait_ge(sems[wsem], wval)
                        inst = fn(eng)
                        inst.then_inc(sems[sem], inc)
                    if engname == "scalar":
                        for s_, v_ in final_dst.items():
                            if v_:
                                eng.wait_ge(sems[s_], v_)
                getattr(block, engname)(engfn)

    return nc


_NC_CACHE = [None]


def kernel(L0_attn_x, L0_attn_Wq, L0_attn_bq, L0_attn_Wk, L0_attn_bk,
           L0_attn_Wv, L0_attn_bv, L0_attn_Wo, L0_attn_bo, L0_attn_proj,
           L0_ffn_weight, L0_ffn_bias, L0_c, L0_wmod):
    x = np.asarray(L0_attn_x)
    assert x.shape == (8, T, D)

    def colsof(b):
        return np.ascontiguousarray(np.asarray(b, np.float32).reshape(8, 128).T)

    common = dict(
        wqt=np.ascontiguousarray(np.asarray(L0_attn_Wq, np.float32).T),
        wkt=np.ascontiguousarray(np.asarray(L0_attn_Wk, np.float32).T),
        wvt=np.ascontiguousarray(np.asarray(L0_attn_Wv, np.float32).T),
        wo=np.ascontiguousarray(np.asarray(L0_attn_Wo, np.float32)),
        wffn=np.ascontiguousarray(np.asarray(L0_ffn_weight, np.float32)),
        wmod=np.ascontiguousarray(np.asarray(L0_wmod, np.float32)),
        projt=np.ascontiguousarray(np.asarray(L0_attn_proj, np.float32).T),
        bqc=colsof(L0_attn_bq), bkc=colsof(L0_attn_bk), bvc=colsof(L0_attn_bv),
        boc=colsof(L0_attn_bo), bffc=colsof(L0_ffn_bias),
        idb=np.eye(128, dtype=ml_dtypes.bfloat16),
        idf8=np.eye(8, dtype=np.float32),
        maskt=np.triu(np.ones((128, 128), dtype=np.float32)),
        ones1b=np.ones((1, 128), dtype=ml_dtypes.bfloat16),
        ones1f=np.ones((1, 128), dtype=np.float32),
    )
    negblk = np.zeros((128, 8, 16), dtype=np.float32)
    for oc in range(8):
        negblk[0:64, oc, 2 * oc] = -1.0 / 16.0
        negblk[64:128, oc, 2 * oc + 1] = -1.0 / 16.0
    common["negblk"] = negblk.astype(ml_dtypes.bfloat16)

    cmat = np.asarray(L0_c, np.float32)
    in_maps = []
    for b in range(8):
        m = dict(common)
        m["x"] = np.ascontiguousarray(np.asarray(x[b], np.float32))
        m["crow"] = np.ascontiguousarray(cmat[b].reshape(8, 128))
        in_maps.append(m)

    nc = build_program()
    res = run_bass_kernel_spmd(nc, in_maps, core_ids=list(range(8)))
    out = np.stack([np.asarray(r["y"], np.float32) for r in res.results], axis=0)
    return out


# version 16 (final)
# speedup vs baseline: 1.0164x; 1.0088x over previous
import contextlib
import math

import numpy as np
import ml_dtypes

import concourse.bass as bass
import concourse.mybir as mybir
from concourse.bass_utils import run_bass_kernel_spmd

F32 = mybir.dt.float32
BF16 = mybir.dt.bfloat16
AF = mybir.ActivationFunctionType
ALU = mybir.AluOpType
AX = mybir.AxisListType

D = 1024
H = 16
DK = 64
MF = 128
T = 4096
CH = 128
TILE = 512
NT = T // TILE
STAGE = 99
PSKIP = ""
LAST_P = None
LOWER = False
NCH = TILE // CH
EPS = 1e-6
LN_EPS = 1e-5
LNM = math.log(math.sqrt(MF))
DKQ = float(DK) ** -0.25


class P:
    def __init__(self):
        self.steps = []
        self.counts = {}

    def emit(self, engine, fn, waits=(), dma=False, dsem=None):
        if engine == "sync":
            dma = True
        sem = {"sync": "dx", "vector": "ve", "scalar": "sc",
               "tensor": "te", "gpsimd": "gp"}[engine]
        if dma:
            assert dsem is not None, "DMA steps must name their semaphore"
            sem = dsem
        inc = 16 if dma else 1
        self.counts[sem] = self.counts.get(sem, 0) + inc
        ws = []
        for w in waits:
            if w is None:
                continue
            ws.append(w)
        agg = {}
        for s_, v_ in ws:
            agg[s_] = max(agg.get(s_, 0), v_)
        ws = sorted(agg.items())
        self.steps.append((engine, fn, ws, sem, inc))
        return (sem, self.counts[sem])


class G:

    def __init__(self):
        self.w = None
        self.r = []

    def wr(self):
        out = list(self.r)
        if self.w is not None:
            out.append(self.w)
        return out

    def did_w(self, evt):
        self.w = evt
        self.r = []

    def did_r(self, evt):
        self.r.append(evt)


def build_program():
    nc = bass.Bass("TRN2", target_bir_lowering=LOWER, debug=False)

    def din(name, shape, dt=F32):
        return nc.dram_tensor(name, shape, dt, kind="ExternalInput").ap()

    x_h = din("x", [T, D])
    wqt_h = din("wqt", [D, D])
    wkt_h = din("wkt", [D, D])
    wvt_h = din("wvt", [D, D])
    wo_h = din("wo", [D, D])
    wffn_h = din("wffn", [D, D])
    wmod_h = din("wmod", [D, 6 * D])
    projt_h = din("projt", [DK, MF])
    crow_h = din("crow", [8, 128])
    bqc_h = din("bqc", [128, 8])
    bkc_h = din("bkc", [128, 8])
    bvc_h = din("bvc", [128, 8])
    boc_h = din("boc", [128, 8])
    bffc_h = din("bffc", [128, 8])
    idb_h = din("idb", [128, 128], BF16)
    idf8_h = din("idf8", [8, 8], F32)
    negblk_h = din("negblk", [128, 8, 16], BF16)
    maskt_h = din("maskt", [128, 128], F32)
    ones1b_h = din("ones1b", [1, 128], BF16)
    ones1f_h = din("ones1f", [1, 128], F32)

    y_h = nc.dram_tensor("y", [T, D], F32, kind="ExternalOutput").ap()

    scr_sq = nc.dram_tensor("scr_sq", [2, 2, H * TILE], BF16, kind="Internal").ap()
    scr_rowb = nc.dram_tensor("scr_rowb", [3, D], BF16, kind="Internal").ap()
    scr_rowf = nc.dram_tensor("scr_rowf", [2, D], F32, kind="Internal").ap()

    pr = P()
    global LAST_P
    LAST_P = pr
    EM = pr.emit

    def CH(engine, fns, waits=()):
        ev = None
        for i, f in enumerate(fns):
            w = list(waits) + ([ev] if ev else [])
            ev = EM(engine, f, waits=w)
        return ev

    ctx = contextlib.ExitStack()
    sbuf = lambda name, shape, dt=F32: ctx.enter_context(nc.sbuf_tensor(name, shape, dt))
    psum = lambda name, shape, dt=F32: ctx.enter_context(nc.psum_tensor(name, shape, dt))

    with ctx:
        idb = sbuf("t_idb", [128, 128], BF16)
        idf8 = sbuf("t_idf8", [8, 8], F32)
        negblk = sbuf("t_negblk", [128, 8, 16], BF16)
        maskt = sbuf("t_maskt", [128, 128], F32)
        ones1b = sbuf("t_ones1b", [1, 128], BF16)
        ones1f = sbuf("t_ones1f", [1, 128], F32)
        projt_f = sbuf("projt_f", [DK, MF], F32)
        projtaug = sbuf("projtaug", [66, MF], BF16)
        crow = sbuf("t_crow", [8, 128], F32)
        silu_row = sbuf("silu_row", [8, 128], F32)
        silu_col = sbuf("silu_col", [128, 8], F32)
        sacol = sbuf("sacol", [128, 8], F32)
        sbcol = sbuf("sbcol", [128, 8], F32)
        ca1 = sbuf("ca1", [128, 8], F32)
        cb1c = sbuf("cb1c", [128, 8], F32)
        gac = sbuf("gac", [128, 8], F32)
        gbc = sbuf("gbc", [128, 8], F32)
        tmp8 = sbuf("tmp8", [128, 8], F32)
        tmp8b = sbuf("tmp8b", [128, 8], F32)
        biasc = sbuf("biasc", [128, 40], F32)
        qkvb = sbuf("qkvb", [128, 24], F32)
        vb_bf = sbuf("vb_bf", [128, 8], BF16)
        gabo_bf = sbuf("gabo_bf", [128, 8], BF16)
        ffnb_bf = sbuf("ffnb_bf", [128, 8], BF16)
        vbrow = sbuf("vbrow", [1, D], BF16)
        gaborow = sbuf("gaborow", [1, D], BF16)
        ffnbrow = sbuf("ffnbrow", [1, D], BF16)
        cb1row_f = sbuf("cb1row_f", [1, D], F32)
        sbrow_f = sbuf("sbrow_f", [1, D], F32)
        cb1_rep = sbuf("cb1_rep", [128, D], F32)
        sb_rep = sbuf("sb_rep", [128, D], F32)

        wq_sb = sbuf("wq_sb", [128, 8, D], BF16)
        wk_sb = sbuf("wk_sb", [128, 8, D], BF16)
        wv_sb = sbuf("wv_sb", [128, 8, D], BF16)
        wot_sb = sbuf("wot_sb", [128, 8, D], BF16)
        wdt_sb = sbuf("wdt_sb", [128, 8, D], BF16)

        xb = sbuf("xb", [128, 4, D], F32)

        pool1 = sbuf("pool1", [128, 8192], BF16)
        pv1f = pool1[:].bitcast(F32)
        stage0 = pv1f[:, 0:1024]
        stage1 = pv1f[:, 1024:2048]
        hn = pool1[:, 0:1024]
        aug = pool1[0:66, :].rearrange("p (h x) -> p h x", h=16)
        scr_c = pv1f[:, 0:1024]
        scr_d = pv1f[:, 1024:2048]
        scr_e = pv1f[:, 2048:3072]
        scr_g = pv1f[:, 3072:4096]

        pool2 = sbuf("pool2", [128, 4096], BF16)
        hnT = pool2[:].rearrange("p (k x) -> p k x", k=8)
        pv2f = pool2[:].bitcast(F32)
        gq2 = pv2f[:, 0:1024]
        gq4 = pv2f[:, 1024:2048]

        pool3 = sbuf("pool3", [128, 1024], F32)
        st_ = pool3[:]
        pv3b = pool3[:].bitcast(BF16)
        q2scr = pv3b[:, 0:1024].rearrange("p (s x) -> p s x", s=2)
        sq_f = pool3[0:16, 512:1024]
        sq_hi = pv3b[0:16, 0:512]
        sq_lo = pv3b[0:16, 512:1024]

        pool4 = sbuf("pool4", [128, 1024], BF16)
        atto = pool4[:]
        attoT = pool4[:].rearrange("p (k x) -> p k x", k=8)

        pool5 = sbuf("pool5", [128, 1024], BF16)
        xd = pool5[:]
        xdT = pool5[:].rearrange("p (k x) -> p k x", k=8)

        h2t = sbuf("h2t", [128, D], F32)
        qf = sbuf("qf", [128, H, TILE], BF16)
        kf = sbuf("kf", [128, H, TILE], BF16)
        vaug = sbuf("vaug", [128, 4, H, 66], BF16)
        ssnap = sbuf("ssnap", [128, H, 66], BF16)
        kftok = sbuf("kftok", [128, 2, 4, 128], BF16)
        am = sbuf("am", [128, 2, 4, 128], BF16)
        dentmp = sbuf("dentmp", [128, 4], F32)
        rden = sbuf("rden", [128, 4], F32)

        mv6 = sbuf("mv6", [128, 12], F32)
        mvar = sbuf("mvar", [128, 4, 2], F32)
        meanv = sbuf("meanv", [128, 4], F32)
        varv = sbuf("varv", [128, 4], F32)
        rstd = sbuf("rstd", [128, 4], F32)
        lnv = sbuf("lnv", [128, 4], F32)
        lneps = sbuf("lneps", [128, 1], F32)
        epsb = sbuf("epsb", [128, 1], F32)
        tch = sbuf("tch", [1, 4], F32)

        amax = sbuf("amax", [128, 32], F32)
        acc1 = sbuf("acc1", [128, 1], F32)
        scq = sbuf("scq", [128, 32], F32)
        rscq = sbuf("rscq", [128, 32], F32)
        wstage = sbuf("wstage", [128, D], BF16)

        ps_s1 = psum("ps_s1", [128, 8, 64])
        ps_s2 = psum("ps_s2", [128, 8, 64])
        ps_z = psum("ps_z", [128, 512])
        ps_a = psum("ps_a", [128, 512])
        ps_b = psum("ps_b", [128, 512])
        ps_c = psum("ps_c", [128, 512])
        ps_d = psum("ps_d", [128, 512])
        ps_e = psum("ps_e", [128, 512])

        sems = {n: ctx.enter_context(nc.semaphore(n))
                for n in ("dcst", "sg0", "sg1", "dx0", "dx1", "dx2", "dx3",
                          "dst0", "dst1", "dst2", "dst3", "dqh", "dql", "dqb3",
                          "dqp", "ve", "sc", "te", "gp")}

        gA, gB, gC, gD, gE = G(), G(), G(), G(), G()
        g_p1 = G()
        g_p2 = G()
        g_p3 = G()
        g_p4 = G()
        g_p5 = G()
        g_qf, g_kf = G(), G()
        g_vaug = [G(), G(), G(), G()]
        g_ssnap = G()
        g_kt = [G(), G()]
        g_am = [G(), G()]
        g_wst = G()
        g_xb = [G(), G(), G(), G()]
        g_h2 = G()
        g_sq8 = G()

        def ld(dst, src, slow=False):
            if slow:
                return lambda e: e.dma_start(out=dst, in_=src, allow_slow_non_contiguous=True)
            return lambda e: e.dma_start(out=dst, in_=src)

        e_c = None
        for dst, src in [(idb[:], idb_h), (idf8[:], idf8_h), (negblk[:], negblk_h),
                         (maskt[:], maskt_h), (ones1b[:], ones1b_h), (ones1f[:], ones1f_h),
                         (projt_f[:], projt_h), (crow[:], crow_h),
                         (biasc[:, 0:8], bqc_h), (biasc[:, 8:16], bkc_h),
                         (biasc[:, 16:24], bvc_h), (biasc[:, 24:32], boc_h),
                         (biasc[:, 32:40], bffc_h)]:
            e_c = EM("sync", ld(dst, src), dsem="dcst")
        e_consts = e_c

        def s_pta(e):
            e.tensor_scalar(projtaug[0:64, :], projt_f[:], DKQ, None, ALU.mult)
            return e.memset(projtaug[64:66, :], 1.0)
        e_pta = EM("vector", s_pta, waits=[e_consts])

        e_silu = EM("scalar", lambda e: e.activation(silu_row[:], crow[:], AF.Silu),
                    waits=[e_consts])
        e_silu_t = EM("tensor",
                      lambda e: e.matmul(ps_a[:, 0:8], silu_row[:], idf8[:], start=True, stop=True),
                      waits=[e_silu, e_consts] + gA.wr())
        gA.did_w(e_silu_t)
        e_silu_c = EM("vector", lambda e: e.tensor_copy(silu_col[:], ps_a[:, 0:8]),
                      waits=[e_silu_t])
        gA.did_r(e_silu_c)

        OBLK = [1, 0, 2, 4, 3, 5]
        qf_flat = qf[:].rearrange("p a b -> p (a b)").bitcast(F32)
        stage0 = qf_flat[:, 0:1024]
        stage1 = qf_flat[:, 1024:2048]
        wslot = [stage0, stage1]
        wguard = [G(), G()]
        first_mm = [True]
        e_wm = {}
        e_modw = EM if True else None
        gB_writer_started = False
        for i, ob in enumerate(OBLK):
            if "stg" in PSKIP:
                break
            for kc in range(8):
                sl = (i * 8 + kc) % 2
                e_dma = EM("sync", ld(wslot[sl][:], wmod_h[kc * 128:(kc + 1) * 128, ob * D:(ob + 1) * D]),
                           waits=wguard[sl].wr(), dsem=("sg0", "sg1")[sl])
                wguard[sl].did_w(e_dma)
                def s_mm(sl=sl, ob=ob, kc=kc, first=first_mm[0]):
                    def f(e):
                        r = None
                        for oc in range(8):
                            r = e.matmul(ps_b[:, ob * 8 + oc: ob * 8 + oc + 1],
                                         wslot[sl][:, oc * 128:(oc + 1) * 128],
                                         silu_col[:, kc:kc + 1],
                                         start=(first and oc == 0), stop=True,
                                         skip_group_check=True)
                        return r
                    return f
                if "mod" in PSKIP:
                    e_wm[(ob, kc)] = e_dma
                    continue
                w = [e_dma, e_silu_c]
                if first_mm[0]:
                    w += gB.wr()
                e_mm = EM("tensor", s_mm(), waits=w)
                if first_mm[0]:
                    gB.did_w(e_mm)
                first_mm[0] = False
                gB.w = e_mm
                wguard[sl].did_r(e_mm)
                e_wm[(ob, kc)] = e_mm

        def mod_ok(ob):
            return e_wm.get((ob, 7), e_consts)

        e_ca1 = EM("vector", lambda e: e.tensor_single_scalar(ca1[:], ps_b[:, 8:16], 1.0, ALU.add),
                   waits=[mod_ok(1)])
        gB.did_r(e_ca1)
        e_sa = EM("vector", lambda e: e.tensor_copy(sacol[:], ps_b[:, 0:8]),
                  waits=[mod_ok(0)])
        gB.did_r(e_sa)
        e_ga_cp = EM("vector", lambda e: e.tensor_copy(gac[:], ps_b[:, 16:24]),
                     waits=[mod_ok(2)])
        gB.did_r(e_ga_cp)
        e_ga_sgn = EM("scalar", lambda e: e.activation(tmp8[:], gac[:], AF.Sign),
                      waits=[e_ga_cp])
        e_ga = CH("vector", [
            lambda e: e.scalar_tensor_tensor(gac[:], gac[:], -1.0, gac[:], ALU.mult, ALU.max),
            lambda e: e.tensor_single_scalar(gac[:], gac[:], 0.25, ALU.max),
            lambda e: e.tensor_mul(gac[:], gac[:], tmp8[:]),
        ], waits=[e_ga_sgn])
        gB.did_r(e_ga)

        def s_cb1(e):
            e.tensor_single_scalar(cb1c[:], ps_b[:, 32:40], 1.0, ALU.add)
            return e.tensor_copy(sbcol[:], ps_b[:, 24:32])
        e_cb1 = EM("vector", s_cb1, waits=[mod_ok(4), mod_ok(3)])
        gB.did_r(e_cb1)

        e_gb_cp = EM("vector", lambda e: e.tensor_copy(gbc[:], ps_b[:, 40:48]),
                     waits=[mod_ok(5)])
        gB.did_r(e_gb_cp)
        if "sgn" in PSKIP:
            e_gb_sgn = e_gb_cp
        else:
            e_gb_sgn = EM("scalar", lambda e: e.activation(tmp8b[:], gbc[:], AF.Sign),
                          waits=[e_gb_cp])

        e_gb = CH("vector", [
            lambda e: e.scalar_tensor_tensor(gbc[:], gbc[:], -1.0, gbc[:], ALU.mult, ALU.max),
            lambda e: e.tensor_single_scalar(gbc[:], gbc[:], 0.25, ALU.max),
            lambda e: e.tensor_mul(gbc[:], gbc[:], tmp8b[:]),
        ], waits=[e_gb_sgn])
        gB.did_r(e_gb)

        first_qb = [True]
        e_wcast = {}
        for wi, (wh, wsb) in enumerate([(wqt_h, wq_sb), (wkt_h, wk_sb), (wvt_h, wv_sb)]):
            if "stg" in PSKIP:
                e_wcast[(wi, 7)] = e_consts
                continue
            for kc in range(8):
                sl = (wi * 8 + kc) % 2
                e_dma = EM("sync", ld(wslot[sl][:], wh[kc * 128:(kc + 1) * 128, :]),
                           waits=wguard[sl].wr(), dsem=("sg0", "sg1")[sl])
                wguard[sl].did_w(e_dma)
                def s_qb(sl=sl, wi=wi, kc=kc, first=first_qb[0]):
                    def f(e):
                        r = None
                        for oc in range(8):
                            r = e.matmul(ps_c[:, wi * 8 + oc: wi * 8 + oc + 1],
                                         wslot[sl][:, oc * 128:(oc + 1) * 128],
                                         sacol[:, kc:kc + 1],
                                         start=(first and oc == 0), stop=True,
                                         skip_group_check=True)
                        return r
                    return f
                if "qb" in PSKIP:
                    e_qb = e_dma
                else:
                    w = [e_dma, e_sa]
                    if first_qb[0]:
                        w += gC.wr()
                    e_qb = EM("tensor", s_qb(), waits=w)
                    if first_qb[0]:
                        gC.did_w(e_qb)
                    first_qb[0] = False
                    gC.w = e_qb
                    wguard[sl].did_r(e_qb)
                def s_cast(sl=sl, wsb=wsb, kc=kc):
                    def f(e):
                        return e.tensor_scalar(wsb[:, kc, :], wslot[sl][:], ca1[:, kc:kc + 1],
                                               None, ALU.mult)
                    return f
                if "cv" in PSKIP:
                    e_cv = e_qb
                else:
                    e_cv = EM("vector", s_cast(), waits=[e_dma, e_ca1, e_qb])
                    wguard[sl].did_r(e_cv)
                e_wcast[(wi, kc)] = e_cv
        e_qb_done = pr.steps and ("te", pr.counts["te"])

        def s_qkvb(e):
            e.tensor_add(qkvb[:, 0:8], ps_c[:, 0:8], biasc[:, 0:8])
            e.tensor_add(qkvb[:, 8:16], ps_c[:, 8:16], biasc[:, 8:16])
            return e.tensor_add(qkvb[:, 16:24], ps_c[:, 16:24], biasc[:, 16:24])
        if "qkvb" in PSKIP:
            e_qkvb0 = e_qb_done
            e_qkvb = e_qb_done
        else:
            e_qkvb0 = EM("vector", s_qkvb, waits=[e_qb_done])
            e_qkvb = EM("vector", lambda e: e.tensor_copy(vb_bf[:], qkvb[:, 16:24]),
                        waits=[e_qkvb0])
        gC.did_r(e_qkvb)

        def bounce(dst_row, src_col, dram_row, waits, readers_done):
            if "bounce" in PSKIP:
                readers_done.append(waits[0])
                return waits[0]
            e1 = EM("gpsimd", ld(dram_row.rearrange("(c p) -> p c", p=128), src_col[:], slow=True),
                    waits=waits, dma=True, dsem="dqp")
            EM("gpsimd", lambda e: e.engine_nop(), waits=[e1])
            e2 = EM("gpsimd", ld(dst_row[:], dram_row.unsqueeze(0)), waits=[e1], dma=True, dsem="dqp")
            EM("gpsimd", lambda e: e.engine_nop(), waits=[e2])
            readers_done.append(e2)
            return e2

        dummy = []
        e_vbrow = bounce(vbrow, vb_bf, scr_rowb[0, :], [e_qkvb], dummy)

        e_gabo = CH("vector", [
            lambda e: e.tensor_mul(tmp8[:], gac[:], biasc[:, 24:32]),
            lambda e: e.tensor_copy(gabo_bf[:], tmp8[:]),
        ], waits=[e_ga, e_ga_sgn])
        e_gaborow = bounce(gaborow, gabo_bf, scr_rowb[1, :], [e_gabo], dummy)

        e_ffnb = CH("vector", [
            lambda e: e.tensor_mul(tmp8b[:], gbc[:], biasc[:, 32:40]),
            lambda e: e.tensor_copy(ffnb_bf[:], tmp8b[:]),
        ], waits=[e_gb, e_gb_sgn])
        e_ffnbrow = bounce(ffnbrow, ffnb_bf, scr_rowb[2, :], [e_ffnb], dummy)

        e_cb1f = EM("vector", lambda e: e.tensor_copy(tmp8[:], cb1c[:]),
                    waits=[e_cb1, e_gaborow])
        e_cb1row = bounce(cb1row_f, tmp8, scr_rowf[0, :], [e_cb1f], dummy)
        e_sbf = EM("vector", lambda e: e.tensor_copy(tmp8b[:], sbcol[:]),
                   waits=[e_cb1, e_ffnbrow])
        e_sbrow = bounce(sbrow_f, tmp8b, scr_rowf[1, :], [e_sbf], dummy)

        if "rep" not in PSKIP or "rep1" in PSKIP:
            def s_rep1(e):
                if "rep1k" in PSKIP:
                    return e.matmul(ps_d[:, 0:128], ones1f[:], ones1f[:], start=True, stop=True)
                if "rep1s2" in PSKIP:
                    e.matmul(ps_a[:, 0:8], silu_row[:], idf8[:], start=True, stop=True)
                    return e.matmul(ps_a[:, 0:8], silu_row[:], idf8[:], start=True, stop=True)
                if "rep1s" in PSKIP:
                    return e.matmul(ps_a[:, 0:8], silu_row[:], idf8[:], start=True, stop=True)
                if "rep1t" in PSKIP:
                    return e.engine_nop()
                if "rep1a" in PSKIP:
                    return e.matmul(ps_d[:, :], ones1f[:], cb1row_f[0:1, 0:512], start=True, stop=True)
                if "rep1b" in PSKIP:
                    return e.matmul(ps_e[:, :], ones1f[:], cb1row_f[0:1, 512:1024], start=True, stop=True)
                if "rep1w" in PSKIP:
                    e.matmul(ps_d[:, :], ones1f[:], wstage[0:1, 0:512], start=True, stop=True)
                    return e.matmul(ps_e[:, :], ones1f[:], wstage[0:1, 512:1024], start=True, stop=True)
                if "rep1n" in PSKIP:
                    e.matmul(ps_d[:, 0:128], ones1f[:], cb1row_f[0:1, 0:128], start=True, stop=True)
                    return e.matmul(ps_e[:, 0:128], ones1f[:], cb1row_f[0:1, 512:640], start=True, stop=True)
                e.matmul(ps_d[:, :], ones1f[:], cb1row_f[0:1, 0:512], start=True, stop=True)
                return e.matmul(ps_e[:, :], ones1f[:], cb1row_f[0:1, 512:1024], start=True, stop=True)
            rep1_waits = [e_consts] if "rep1w" in PSKIP else [e_cb1row]
            e_rep1 = EM("tensor", s_rep1, waits=rep1_waits + gD.wr() + gE.wr())
            gD.did_w(e_rep1); gE.did_w(e_rep1)

            def s_rep1v(e):
                e.tensor_copy(cb1_rep[:, 0:512], ps_d[:, :])
                return e.tensor_copy(cb1_rep[:, 512:1024], ps_e[:, :])
            if "rep1" in PSKIP:
                e_rep1v = e_rep1
                e_rep2v = e_rep1
            else:
                e_rep1v = EM("vector", s_rep1v, waits=[e_rep1])
                gD.did_r(e_rep1v); gE.did_r(e_rep1v)

            if "rep1" not in PSKIP:
                def s_rep2(e):
                    e.matmul(ps_d[:, :], ones1f[:], sbrow_f[0:1, 0:512], start=True, stop=True)
                    return e.matmul(ps_e[:, :], ones1f[:], sbrow_f[0:1, 512:1024], start=True, stop=True)
                e_rep2 = EM("tensor", s_rep2, waits=[e_sbrow] + gD.wr() + gE.wr())
                gD.did_w(e_rep2); gE.did_w(e_rep2)

                def s_rep2v(e):
                    e.tensor_copy(sb_rep[:, 0:512], ps_d[:, :])
                    return e.tensor_copy(sb_rep[:, 512:1024], ps_e[:, :])
                e_rep2v = EM("vector", s_rep2v, waits=[e_rep2])
                gD.did_r(e_rep2v); gE.did_r(e_rep2v)

        else:
            e_rep1v = e_sbrow
            e_rep2v = e_sbrow

        def quant(src_f32, dst_bf16, col_scale, wfirst, g_dst, g_src=None):
            call_w = list(wfirst) + g_p1.wr() + g_p2.wr() + g_p3.wr() + g_dst.wr()
            per_half = []
            for hh in (0, 1):
                sl = slice(hh * 512, (hh + 1) * 512)
                sq_ = slice(hh * 16, (hh + 1) * 16)
                X = src_f32[:, sl]
                DST = dst_bf16[:, sl]
                C = scr_c[:, sl]; D = scr_d[:, sl]; E = scr_e[:, sl]; G = scr_g[:, sl]
                G2 = gq2[:, sl]; G4 = gq4[:, sl]; S = st_[:, sl]
                AM = amax[:, sq_]; SQ = scq[:, sq_]; RQ = rscq[:, sq_]
                Xb = X.rearrange("p (b j) -> p b j", j=32)
                Cb = C.rearrange("p (b j) -> p b j", j=32)
                Gb = G.rearrange("p (b j) -> p b j", j=32)
                RQb = RQ.unsqueeze(2).broadcast_to([128, 16, 32])
                SQb = SQ.unsqueeze(2).broadcast_to([128, 16, 32])
                st = []
                A = st.append
                A(("vector", lambda e, X=Xb, AM=AM: e.tensor_reduce(
                    AM, X, AX.X, ALU.max, apply_absolute_value=True), []))
                A(("vector", lambda e, AM=AM, SQ=SQ: e.tensor_scalar(
                    SQ, AM, 1.0 / 7.5, 1e-30, ALU.mult, ALU.max), [0]))
                A(("vector", lambda e, SQ=SQ, RQ=RQ: e.reciprocal(RQ, SQ), [1]))
                A(("scalar", lambda e, X=X, C=C: e.activation(
                    C, X, AF.Abs), [0]))
                A(("vector", lambda e, Cb=Cb, RQb=RQb: e.tensor_tensor(
                    Cb, Cb, RQb, ALU.mult), [2, 3]))
                A(("vector", lambda e, C=C, G2=G2: e.tensor_single_scalar(
                    G2, C, 2.0, ALU.is_ge), [4]))
                A(("vector", lambda e, C=C, G4=G4: e.tensor_single_scalar(
                    G4, C, 4.0, ALU.is_ge), [4]))
                A(("vector", lambda e, G2=G2, S=S: e.tensor_scalar(
                    S, G2, 0.125, 0.125, ALU.mult, ALU.add), [5]))
                A(("vector", lambda e, G4=G4, S=S: e.scalar_tensor_tensor(
                    S, G4, 0.25, S, ALU.mult, ALU.add), [6, 7]))
                A(("vector", lambda e, G2=G2, D=D: e.tensor_scalar(
                    D, G2, -4.0, 8.0, ALU.mult, ALU.add), [5]))
                A(("vector", lambda e, G4=G4, D=D: e.scalar_tensor_tensor(
                    D, G4, -2.0, D, ALU.mult, ALU.add), [6, 9]))
                A(("vector", lambda e, C=C, D=D: e.tensor_tensor(
                    D, D, C, ALU.mult), [4, 10]))
                A(("vector", lambda e, D=D, E=E: e.tensor_copy(E, D), [11]))
                prev = 12
                for k in (16.0, 8.0, 4.0, 2.0, 1.0):
                    A(("vector", lambda e, D=D, C=C, k=k: e.tensor_single_scalar(
                        C, D, k - 0.5, ALU.is_ge), [prev]))
                    prev = len(st) - 1
                    A(("vector", lambda e, D=D, C=C, k=k: e.scalar_tensor_tensor(
                        D, C, -k, D, ALU.mult, ALU.add), [prev]))
                    prev = len(st) - 1
                A(("vector", lambda e, E=E, D=D: e.tensor_tensor(
                    E, E, D, ALU.subtract), [12, prev]))
                A(("vector", lambda e, E=E, S=S: e.tensor_tensor(
                    E, E, S, ALU.mult), [8, len(st) - 1]))
                A(("scalar", lambda e, X=X, G=G: e.activation(
                    G, X, AF.Sign), [0]))
                A(("vector", lambda e, Gb=Gb, SQb=SQb: e.tensor_tensor(
                    Gb, Gb, SQb, ALU.mult), [1, len(st) - 1]))
                if col_scale is not None:
                    A(("vector", lambda e, G=G: e.tensor_scalar(
                        G, G, col_scale, None, ALU.mult), [len(st) - 1]))
                A(("vector", lambda e, DST=DST, E=E, G=G: e.tensor_tensor(
                    DST, E, G, ALU.mult), [len(st) - 4 if col_scale is not None
                                           else len(st) - 3, len(st) - 1]))
                per_half.append(st)
            evs = [{}, {}]
            n = len(per_half[0])
            for i in range(n):
                for hh in (0, 1):
                    eng, fn, deps = per_half[hh][i]
                    w = [evs[hh][d] for d in deps]
                    if i == 0:
                        w = w + call_w
                    evs[hh][i] = EM(eng, fn, waits=w)
            finals = [evs[0][n - 1], evs[1][n - 1]]
            for gg in (g_p1, g_p2, g_p3):
                gg.did_w(finals[0])
                gg.did_r(finals[1])
            g_dst.did_w(finals[0])
            g_dst.did_r(finals[1])
            if g_src is not None:
                g_src.did_r(finals[0])
                g_src.did_r(finals[1])
            return finals

        for oc in range(8):
            if "wo" in PSKIP or "stg" in PSKIP:
                break
            sl = oc % 2
            e_dma = EM("sync", ld(wslot[sl][:], wo_h[oc * 128:(oc + 1) * 128, :]),
                       waits=wguard[sl].wr(), dsem=("sg0", "sg1")[sl])
            wguard[sl].did_w(e_dma)
            def s_wsc(sl=sl, oc=oc):
                def f(e):
                    return e.tensor_scalar(wstage[:], wslot[sl][:], gac[:, oc:oc + 1],
                                           None, ALU.mult)
                return f
            e_sc_ = EM("vector", s_wsc(), waits=[e_dma, e_ga] + g_wst.wr())
            g_wst.did_w(e_sc_)
            wguard[sl].did_r(e_sc_)

            for half, bank, gbank in ((0, ps_d, gD), (1, ps_e, gE)):
                def s_tr(half=half, bank=bank):
                    def f(e):
                        r = None
                        for j in range(4):
                            ic = half * 4 + j
                            r = e.matmul(bank[:, j * 128:(j + 1) * 128],
                                         wstage[:, ic * 128:(ic + 1) * 128], idb[:],
                                         start=True, stop=True)
                        return r
                    return f
                e_t = EM("tensor", s_tr(), waits=[e_sc_, e_consts] + gbank.wr())
                gbank.did_w(e_t)
                g_wst.did_r(e_t)
                def s_ev(oc=oc, half=half, bank=bank):
                    def f(e):
                        return e.tensor_copy(
                            wot_sb[:, half * 4:(half + 1) * 4, oc * 128:(oc + 1) * 128],
                            bank[:, 0:512].rearrange("p (k o) -> p k o", k=4))
                    return f
                e_ev = EM("vector", s_ev(), waits=[e_t])
                gbank.did_r(e_ev)
        e_wo_done = ("ve", pr.counts["ve"])

        for oc in range(8):
            if "ffn" in PSKIP or "stg" in PSKIP:
                break
            sl = oc % 2
            e_dma = EM("sync", ld(wslot[sl][:], wffn_h[oc * 128:(oc + 1) * 128, :]),
                       waits=wguard[sl].wr(), dsem=("sg0", "sg1")[sl])
            wguard[sl].did_w(e_dma)
            e_q_l = quant(wslot[sl][:], wstage[:], gbc[:, oc:oc + 1], [e_dma, e_gb],
                          g_wst, g_src=wguard[sl])
            for half, bank, gbank in ((0, ps_d, gD), (1, ps_e, gE)):
                def s_tr2(half=half, bank=bank):
                    def f(e):
                        r = None
                        for j in range(4):
                            ic = half * 4 + j
                            r = e.matmul(bank[:, j * 128:(j + 1) * 128],
                                         wstage[:, ic * 128:(ic + 1) * 128], idb[:],
                                         start=True, stop=True)
                        return r
                    return f
                e_t = EM("tensor", s_tr2(), waits=e_q_l + gbank.wr())
                gbank.did_w(e_t)
                g_wst.did_r(e_t)
                def s_ev2(oc=oc, half=half, bank=bank):
                    def f(e):
                        return e.tensor_copy(
                            wdt_sb[:, half * 4:(half + 1) * 4, oc * 128:(oc + 1) * 128],
                            bank[:, 0:512].rearrange("p (k o) -> p k o", k=4))
                    return f
                e_ev = EM("vector", s_ev2(), waits=[e_t])
                gbank.did_r(e_ev)
        e_wd_done = ("ve", pr.counts["ve"])

        def s_vones(e):
            e.memset(lneps[:], LN_EPS)
            e.memset(epsb[:], EPS)
            e.memset(vaug[:, :, :, 64:65], 1.0)
            e.memset(vaug[:, :, :, 65:66], 0.0)
            return e.memset(ssnap[:], 0.0)
        e_vones = EM("vector", s_vones, waits=[])
        g_ssnap.did_w(e_vones)
        for gg in g_vaug:
            gg.did_w(e_vones)

        if "nan1" in PSKIP:
            CH("vector", [
                lambda e: e.memset(lneps[0:1, 0:1], 0.0),
                lambda e: e.reciprocal(lneps[0:1, 0:1], lneps[0:1, 0:1]),
            ], waits=[e_vones])
        if "nan2" in PSKIP:
            CH("scalar", [
                lambda e: e.activation(lneps[0:1, 0:1], lneps[0:1, 0:1], AF.Exp, scale=1e9),
            ], waits=[e_vones])
        if "nan3" in PSKIP:
            CH("vector", [
                lambda e: e.memset(lneps[0:1, 0:1], 0.0),
                lambda e: e.memset(lneps[1:2, 0:1], 0.0),
                lambda e: e.reciprocal(lneps[0:1, 0:1], lneps[0:1, 0:1]),
                lambda e: e.tensor_tensor(lneps[0:1, 0:1], lneps[0:1, 0:1], lneps[1:2, 0:1], ALU.mult),
            ], waits=[e_vones])

        e_store = {}
        first_s_mm = [True]
        e_kv_last = [None]
        e_tch_last = [None]

        for t in range(NT):
            e_x = {}
            for c in range(NCH):
                g = 4 * t + c
                w = [e_store.get((t - 1, c))] + g_xb[c].wr()
                e_x[c] = EM("sync", ld(xb[:, c, :], x_h[g * 128:(g + 1) * 128, :]), waits=w,
                            dsem=("dx0", "dx1", "dx2", "dx3")[c])
                g_xb[c].did_w(e_x[c])

            e_hnT_ev = {}
            for c in range(NCH):
                if "ln" in PSKIP:
                    e_hnT_ev[c] = e_x[c]
                    continue
                def s_bn(c=c):
                    def f(e):
                        e.bn_stats(mv6[:, 0:6], xb[:, c, 0:512])
                        return e.bn_stats(mv6[:, 6:12], xb[:, c, 512:1024])
                    return f
                e_b1 = CH("vector", [
                    lambda e, c=c: e.bn_stats(mv6[:, 0:6], xb[:, c, 0:512]),
                    lambda e, c=c: e.bn_stats(mv6[:, 6:12], xb[:, c, 512:1024]),
                    lambda e, c=c: e.bn_aggr(mvar[:, c, :], mv6[:]),
                ], waits=[e_x[c]])
                g_xb[c].did_r(e_b1)

                if "lnr" in PSKIP:
                    e_p2 = e_b1
                else:
                    e_p2 = CH("scalar", [
                        lambda e, c=c: e.activation(lnv[:, c:c + 1], mvar[:, c, 1:2], AF.Ln, bias=lneps[:]),
                        lambda e, c=c: e.activation(rstd[:, c:c + 1], lnv[:, c:c + 1], AF.Exp, scale=-0.5),
                    ], waits=[e_b1, e_vones])

                def s_hn(c=c):
                    def f(e):
                        return e.tensor_scalar(hn[:], xb[:, c, :], mvar[:, c, 0:1],
                                               rstd[:, c:c + 1], ALU.subtract, ALU.mult)
                    return f
                e_hnw = EM("vector", s_hn(), waits=[e_p2] + g_p1.wr())
                g_p1.did_w(e_hnw)
                g_xb[c].did_r(e_hnw)

                def s_hnt(c=c):
                    def f(e):
                        r = None
                        for half, bank in ((0, ps_a), (1, ps_b)):
                            for j in range(4):
                                kc = half * 4 + j
                                r = e.matmul(bank[:, j * 128:(j + 1) * 128],
                                             hn[:, kc * 128:(kc + 1) * 128], idb[:],
                                             start=True, stop=True)
                        return r
                    return f
                e_tr = EM("tensor", s_hnt(), waits=[e_hnw] + gA.wr() + gB.wr())
                gA.did_w(e_tr); gB.did_w(e_tr)
                g_p1.did_r(e_tr)

                def s_hne(c=c):
                    def f(e):
                        e.tensor_copy(hnT[:, 0:4, c * 128:(c + 1) * 128],
                                      ps_a[:, 0:512].rearrange("p (k x) -> p k x", k=4))
                        return e.tensor_copy(hnT[:, 4:8, c * 128:(c + 1) * 128],
                                             ps_b[:, 0:512].rearrange("p (k x) -> p k x", k=4))
                    return f
                wts = [e_tr]
                if c == 0:
                    wts += g_p2.wr()
                e_hne = EM("vector", s_hne(), waits=wts)
                if c == 0:
                    g_p2.did_w(e_hne)
                g_p2.w = e_hne
                gA.did_r(e_hne); gB.did_r(e_hne)
                e_hnT_ev[c] = e_hne
            e_hnT_all = e_hnT_ev[3]
            if STAGE < 2:
                continue

            def emit_proj_side(side, wsb, bcol0, scrsq, banks):
                e_evac = {}
                for oc in range(8):
                    bank, gbank = banks[oc % 2]
                    def s_pj(oc=oc, wsb=wsb, bank=bank):
                        def f(e):
                            r = None
                            for kc in range(8):
                                r = e.matmul(bank[:, :], wsb[:, kc, oc * 128:(oc + 1) * 128],
                                             hnT[:, kc, :], start=(kc == 0), stop=(kc == 7))
                            return r
                        return f
                    wts = [e_hnT_all, e_wcast[(side, 7)]] + gbank.wr()
                    e_pj = EM("tensor", s_pj(), waits=wts)
                    gbank.did_w(e_pj)
                    g_p2.did_r(e_pj)
                    if STAGE == 20:
                        gbank.did_r(e_pj)
                        continue

                    def s_ev(oc=oc, bcol0=bcol0, bank=bank):
                        def f(e):
                            e.tensor_scalar(aug[0:64, 2 * oc, :], bank[0:64, :],
                                            qkvb[0:64, bcol0 + oc:bcol0 + oc + 1], None, ALU.add)
                            return e.tensor_scalar(aug[0:64, 2 * oc + 1, :], bank[64:128, :],
                                                   qkvb[64:128, bcol0 + oc:bcol0 + oc + 1], None, ALU.add)
                        return f
                    wts2 = [e_pj]
                    if oc == 0:
                        wts2 += g_p1.wr()
                    e_evv = EM("vector", s_ev(), waits=wts2)
                    if oc == 0:
                        g_p1.did_w(e_evv)
                    g_p1.w = e_evv
                    gbank.did_r(e_evv)
                    e_evac[oc] = e_evv
                    if STAGE == 21:
                        continue

                    if (2100 <= STAGE < 2200 and oc >= STAGE - 2100) or (2300 <= STAGE < 2400 and oc >= 1):
                        gbank.did_r(e_pj)
                        continue
                    def s_q2(oc=oc, bcol0=bcol0, bank=bank):
                        def f(e):
                            dst = q2scr[:, oc % 2, :]
                            bias = qkvb[:, bcol0 + oc:bcol0 + oc + 1]
                            if STAGE == 2301:
                                return e.activation(dst, bank[:, :], AF.Square)
                            if STAGE == 2302:
                                return e.activation(wstage[:, 0:512], bank[:, :], AF.Square, bias=bias)
                            if STAGE == 2305:
                                return e.copy(lneps[:], lneps[:])
                            if STAGE == 2306:
                                return e.activation(wstage[:, 0:512], xb[:, 0, 0:512], AF.Square)
                            if STAGE == 2307:
                                return e.activation(wstage[:, 0:512], ps_e[0:128, 0:512], AF.Square)
                            if STAGE in (2308, 2309, 2313):
                                return e.copy(wstage[:, 0:512], bank[:, :])
                            if STAGE == 2314:
                                return e.activation(wstage[:, 0:512], bank[:, :], AF.Exp)
                            if STAGE == 2315:
                                return e.activation(wstage[:, 0:512], bank[0:128, 0:512], AF.Identity)
                            if STAGE in (2303, 2304):
                                return e.activation(wstage[:, 0:512], bank[:, :], AF.Square)
                            return e.activation(dst, bank[:, :], AF.Square, bias=bias)
                        return f
                    q2w = [e_evv] + ([] if STAGE in (2304, 2305, 2306, 2307, 2308) else g_p3.wr())
                    if "q2nw" in PSKIP:
                        q2w = [e_consts]
                    e_q2 = EM("scalar", s_q2(), waits=q2w)
                    g_p3.did_w(e_q2)
                    if STAGE not in (2304, 2305, 2306, 2307, 2308):
                        gbank.did_r(e_q2)
                    else:
                        gbank.did_r(e_pj)
                    if STAGE == 215 or 2100 <= STAGE < 2200 or 2300 <= STAGE < 2400:
                        continue

                    def s_sqmm(oc=oc):
                        def f(e):
                            return e.matmul(ps_e[0:16, 0:TILE], negblk[:, oc, :],
                                            q2scr[:, oc % 2, :], start=(oc == 0), stop=(oc == 7))
                        return f
                    w3 = [e_q2]
                    if oc == 0:
                        w3 += gE.wr()
                    e_sqm = EM("tensor", s_sqmm(), waits=w3)
                    if oc == 0:
                        gE.did_w(e_sqm)
                    gE.w = e_sqm
                    g_p3.did_r(e_sqm)
                e_sq_last = gE.w
                if STAGE == 22 or "sqf" in PSKIP:
                    return None, e_evac

                e_sqf = CH("vector", [
                    lambda e: e.tensor_single_scalar(sq_f[:], ps_e[0:16, 0:TILE], -LNM, ALU.add),
                    lambda e: e.tensor_copy(sq_hi[:], sq_f[:]),
                    lambda e: e.tensor_tensor(sq_lo[:], sq_f[:], sq_hi[:], ALU.subtract),
                ], waits=[e_sq_last] + g_p3.wr())
                g_p3.did_w(e_sqf)
                gE.did_r(e_sqf)

                def s_b1(scrsq=scrsq):
                    return lambda e: e.dma_start(out=scrsq[0].rearrange("(h x) -> h x", h=16),
                                                 in_=sq_hi[:])
                def s_b2(scrsq=scrsq):
                    return lambda e: e.dma_start(out=scrsq[1].rearrange("(h x) -> h x", h=16),
                                                 in_=sq_lo[:])
                if STAGE == 23:
                    return None, e_evac
                e_b1 = EM("gpsimd", s_b1(), waits=[e_sqf], dma=True, dsem="dqh")
                e_b2 = EM("gpsimd", s_b2(), waits=[e_sqf], dma=True, dsem="dql")
                g_p3.did_r(e_b1); g_p3.did_r(e_b2)

                def s_b3(scrsq=scrsq):
                    return lambda e: e.dma_start(out=aug[64:66, :, :],
                                                 in_=scrsq.rearrange("a (h x) -> a h x", h=16))
                if STAGE == 24:
                    return None, e_evac
                e_b3 = EM("gpsimd", s_b3(), waits=[e_b1, e_b2, g_p1.w], dma=True, dsem="dqb3")
                g_p1.w = e_b3
                return e_b3, e_evac

            def emit_u_side(phidst, g_phi, e_b3, e_evac, banks):
                e_exp = {}
                for h in range(H):
                    bank, gbank = banks[h % 2]
                    def s_u(h=h, bank=bank):
                        def f(e):
                            return e.matmul(bank[:, :], projtaug[:], aug[0:66, h, :],
                                            start=True, stop=True)
                        return f
                    wts = [e_b3, e_evac[h // 2], e_pta] + gbank.wr()
                    e_u = EM("tensor", s_u(), waits=wts)
                    gbank.did_w(e_u)
                    g_p1.did_r(e_u)

                    tw = [e_u] + ([e_tch_last[0]] if e_tch_last[0] else [])
                    def s_tch(bank=bank):
                        def f(e):
                            return e.tensor_copy(tch[0:1, 0:1], bank[0:1, 0:1])
                        return f
                    e_tch = EM("vector", s_tch(), waits=tw)
                    e_tch_last[0] = e_tch
                    gbank.did_r(e_tch)

                    def s_exp(h=h, phidst=phidst, bank=bank):
                        def f(e):
                            return e.activation(phidst[:, h, :], bank[:, :], AF.Exp)
                        return f
                    wts = [e_tch]
                    if h == 0:
                        wts += g_phi.wr()
                    e_ex = EM("scalar", s_exp(), waits=wts)
                    if h == 0:
                        g_phi.did_w(e_ex)
                    g_phi.w = e_ex
                    gbank.did_r(e_ex)
                    e_exp[h] = e_ex

                def s_eps(phidst=phidst):
                    def f(e):
                        return e.activation(
                            phidst[:].rearrange("p a b -> p (a b)"),
                            phidst[:].rearrange("p a b -> p (a b)"),
                            AF.Identity, bias=epsb[:])
                    return f
                e_eps = EM("scalar", s_eps(), waits=[e_exp[H - 1], e_vones])
                g_phi.w = e_eps
                return e_eps

            def emit_v():
                for c in range(NCH):
                    e_vmm_l = []
                    for half, bank, gbank in ((0, ps_a, gA), (1, ps_b, gB)):
                        def s_v(c=c, half=half, bank=bank):
                            def f(e):
                                for kc in range(8):
                                    e.matmul(bank[:, :], hnT[:, kc, c * 128:(c + 1) * 128],
                                             wv_sb[:, kc, half * 512:(half + 1) * 512],
                                             start=(kc == 0), stop=False)
                                return e.matmul(bank[:, :], ones1b[:],
                                                vbrow[0:1, half * 512:(half + 1) * 512],
                                                start=False, stop=True)
                            return f
                        wts = [e_hnT_ev[c], e_vbrow, e_wcast[(2, 7)]] + gbank.wr()
                        e_vmm = EM("tensor", s_v(), waits=wts)
                        gbank.did_w(e_vmm)
                        g_p2.did_r(e_vmm)
                        e_vmm_l.append(e_vmm)

                    def s_vev(c=c):
                        def f(e):
                            e.tensor_copy(vaug[:, c, 0:8, 0:64],
                                          ps_a[:, 0:512].rearrange("p (h d) -> p h d", h=8))
                            return e.tensor_copy(vaug[:, c, 8:16, 0:64],
                                                 ps_b[:, 0:512].rearrange("p (h d) -> p h d", h=8))
                        return f
                    e_vev = EM("vector", s_vev(), waits=e_vmm_l + g_vaug[c].wr())
                    g_vaug[c].did_w(e_vev)
                    gA.did_r(e_vev); gB.did_r(e_vev)

            pj_banks = ((ps_a, gA), (ps_b, gB)) if "swapbank" in PSKIP else ((ps_c, gC), (ps_d, gD))
            e_b3q, e_evq = emit_proj_side(0, wq_sb, 0, scr_sq[0], pj_banks)
            if STAGE < 3 or STAGE in (20, 21, 215, 22, 23, 24, 25) or 2100 <= STAGE < 2200 or 2300 <= STAGE < 2400:
                continue
            emit_v()
            if STAGE < 4:
                continue
            e_qf_eps = emit_u_side(qf, g_qf, e_b3q, e_evq, ((ps_c, gC), (ps_d, gD)))
            if STAGE < 5:
                continue
            e_b3k, e_evk = emit_proj_side(1, wk_sb, 8, scr_sq[1], ((ps_a, gA), (ps_b, gB)))
            e_kf_eps = emit_u_side(kf, g_kf, e_b3k, e_evk, ((ps_c, gC), (ps_d, gD)))
            if STAGE < 6:
                continue

            for c in range(NCH):
                g = 4 * t + c
                if g > 0:
                    def s_snap(e):
                        e.tensor_copy(ssnap[:, 0:8, 0:64], ps_s1[:, :, :])
                        e.tensor_copy(ssnap[:, 8:16, 0:64], ps_s2[:, :, :])
                        return e.tensor_copy(ssnap[:, :, 64:65], ps_z[:, 0:16].unsqueeze(2))
                    e_snap = EM("vector", s_snap, waits=[e_kv_last[0]] + g_ssnap.wr())
                    g_ssnap.did_w(e_snap)
                else:
                    e_snap = e_vones

                e_div = {}
                for grp in range(4):
                    hs = [4 * grp + i for i in range(4)]
                    par = grp % 2
                    abank, gab = ((ps_a, gA), (ps_c, gC))[par]
                    ndbank, gnd = ((ps_e, gE), (ps_b, gB))[par]

                    def s_amm(hs=hs, c=c, abank=abank):
                        def f(e):
                            r = None
                            for i, h in enumerate(hs):
                                r = e.matmul(abank[:, i * 128:(i + 1) * 128],
                                             kf[:, h, c * 128:(c + 1) * 128],
                                             qf[:, h, c * 128:(c + 1) * 128],
                                             start=True, stop=True)
                            return r
                        return f
                    e_amm = EM("tensor", s_amm(), waits=[e_qf_eps, e_kf_eps] + gab.wr())
                    gab.did_w(e_amm)
                    g_qf.did_r(e_amm); g_kf.did_r(e_amm)

                    def s_amask(par=par, abank=abank):
                        def f(e):
                            return e.tensor_tensor(
                                am[:, par, :, :],
                                abank[:, 0:512].rearrange("p (i x) -> p i x", i=4),
                                maskt[:].unsqueeze(1).broadcast_to([128, 4, 128]), ALU.mult)
                        return f
                    e_amk = EM("vector", s_amask(), waits=[e_amm] + g_am[par].wr())
                    g_am[par].did_w(e_amk)
                    gab.did_r(e_amk)

                    def s_ktr(hs=hs, c=c):
                        def f(e):
                            r = None
                            for i, h in enumerate(hs):
                                r = e.matmul(ps_d[:, i * 128:(i + 1) * 128],
                                             kf[:, h, c * 128:(c + 1) * 128], idb[:],
                                             start=True, stop=True)
                            return r
                        return f
                    e_ktr = EM("tensor", s_ktr(), waits=[e_kf_eps] + gD.wr())
                    gD.did_w(e_ktr)
                    g_kf.did_r(e_ktr)

                    def s_ktev(par=par):
                        def f(e):
                            return e.tensor_copy(kftok[:, par, :, :],
                                                 ps_d[:, 0:512].rearrange("p (i x) -> p i x", i=4))
                        return f
                    e_ktev = EM("vector", s_ktev(), waits=[e_ktr] + g_kt[par].wr())
                    g_kt[par].did_w(e_ktev)
                    gD.did_r(e_ktev)

                    def s_nd(hs=hs, c=c, g=g, par=par, ndbank=ndbank):
                        def f(e):
                            r = None
                            for i, h in enumerate(hs):
                                r = e.matmul(ndbank[:, i * 66:i * 66 + 65],
                                             am[:, par, i, :], vaug[:, c, h, 0:65],
                                             start=True, stop=(g == 0),
                                             skip_group_check=True)
                                if g > 0:
                                    r = e.matmul(ndbank[:, i * 66:i * 66 + 65],
                                                 qf[:, h, c * 128:(c + 1) * 128],
                                                 ssnap[:, h, 0:65],
                                                 start=False, stop=True,
                                                 skip_group_check=True)
                            return r
                        return f
                    e_nd = EM("tensor", s_nd(),
                              waits=[e_amk, e_snap, g_vaug[c].w] + gnd.wr())
                    gnd.did_w(e_nd)
                    g_am[par].did_r(e_nd)
                    g_vaug[c].did_r(e_nd)
                    g_ssnap.did_r(e_nd)
                    g_qf.did_r(e_nd)

                    def s_kv(hs=hs, c=c, par=par, first=first_s_mm[0]):
                        def f(e):
                            r = None
                            for i, h in enumerate(hs):
                                sbank = ps_s1 if h < 8 else ps_s2
                                r = e.matmul(sbank[:, h % 8, :], kftok[:, par, i, :],
                                             vaug[:, c, h, 0:64],
                                             start=(first and h in (0, 8)),
                                             stop=True, skip_group_check=True)
                                r = e.matmul(ps_z[:, h:h + 1], kftok[:, par, i, :],
                                             vaug[:, c, h, 64:65],
                                             start=(first and h == 0),
                                             stop=True, skip_group_check=True)
                            return r
                        return f
                    e_kv = EM("tensor", s_kv(), waits=[e_ktev, e_snap])
                    if grp == 3:
                        first_s_mm[0] = False
                    g_kt[par].did_r(e_kv)
                    g_vaug[c].did_r(e_kv)
                    e_kv_last[0] = e_kv

                    wts = [e_nd]
                    if grp == 0:
                        wts += g_p4.wr()
                    e_dv = CH("vector", [
                        lambda e, ndbank=ndbank: e.tensor_single_scalar(
                            dentmp[:],
                            ndbank[:, 0:264].rearrange("p (i x) -> p i x", i=4)[:, :, 64:65].squeeze(2),
                            EPS, ALU.add),
                        lambda e: e.reciprocal(rden[:], dentmp[:]),
                        lambda e, grp=grp, ndbank=ndbank: e.tensor_tensor(
                            atto[:, grp * 256:(grp + 1) * 256].rearrange("p (i d) -> p i d", i=4),
                            ndbank[:, 0:264].rearrange("p (i x) -> p i x", i=4)[:, :, 0:64],
                            rden[:].unsqueeze(2).broadcast_to([128, 4, 64]), ALU.mult),
                    ], waits=wts)
                    if grp == 0:
                        g_p4.did_w(e_dv)
                    g_p4.w = e_dv
                    gnd.did_r(e_dv)
                    e_div[grp] = e_dv

                if STAGE < 7:
                    continue
                def s_aot(e):
                    r = None
                    for half, bank in ((0, ps_a), (1, ps_c)):
                        for j in range(4):
                            kc = half * 4 + j
                            r = e.matmul(bank[:, j * 128:(j + 1) * 128],
                                         atto[:, kc * 128:(kc + 1) * 128], idb[:],
                                         start=True, stop=True)
                    return r
                e_aot = EM("tensor", s_aot,
                           waits=[e_div[0], e_div[1], e_div[2], e_div[3]] + gA.wr() + gC.wr())
                gA.did_w(e_aot); gC.did_w(e_aot)
                g_p4.did_r(e_aot)

                def s_aoev(e):
                    e.tensor_copy(attoT[:, 0:4, :], ps_a[:, 0:512].rearrange("p (k x) -> p k x", k=4))
                    return e.tensor_copy(attoT[:, 4:8, :], ps_c[:, 0:512].rearrange("p (k x) -> p k x", k=4))
                e_aoev = EM("vector", s_aoev, waits=[e_aot] + g_p4.wr())
                g_p4.did_w(e_aoev)
                gA.did_r(e_aoev); gC.did_r(e_aoev)

                e_wo = {}
                for half, bank, gbank in ((0, ps_d, gD), (1, ps_e, gE)):
                    def s_wo(half=half, bank=bank):
                        def f(e):
                            for kc in range(8):
                                e.matmul(bank[:, :], attoT[:, kc, :],
                                         wot_sb[:, kc, half * 512:(half + 1) * 512],
                                         start=(kc == 0), stop=False)
                            return e.matmul(bank[:, :], ones1b[:],
                                            gaborow[0:1, half * 512:(half + 1) * 512],
                                            start=False, stop=True)
                        return f
                    wts = [e_aoev, e_gaborow, e_wo_done] + gbank.wr()
                    e_w = EM("tensor", s_wo(), waits=wts)
                    gbank.did_w(e_w)
                    g_p4.did_r(e_w)
                    e_wo[half] = e_w

                e_epi_l = []
                for half, bank, gbank in ((0, ps_d, gD), (1, ps_e, gE)):
                    def s_wep(c=c, half=half, bank=bank):
                        def f(e):
                            return e.tensor_tensor(
                                xb[:, c, half * 512:(half + 1) * 512],
                                xb[:, c, half * 512:(half + 1) * 512],
                                bank[:, :], ALU.add)
                        return f
                    e_ep = EM("vector", s_wep(), waits=[e_wo[half]])
                    gbank.did_r(e_ep)
                    g_xb[c].did_r(e_ep)
                    g_xb[c].w = e_ep
                    e_epi_l.append(e_ep)
                e_epi = e_epi_l[1]

                if STAGE < 8:
                    continue
                e2_b0 = EM("vector",
                           lambda e, c=c: e.bn_stats(mv6[:, 0:6], xb[:, c, 0:512]),
                           waits=[e_epi_l[0]])
                e2_b0b = EM("vector",
                            lambda e, c=c: e.bn_stats(mv6[:, 6:12], xb[:, c, 512:1024]),
                            waits=[e_epi_l[1]])
                e2_b1 = EM("vector",
                           lambda e, c=c: e.bn_aggr(mvar[:, c, :], mv6[:]),
                           waits=[e2_b0, e2_b0b])
                g_xb[c].did_r(e2_b0)
                g_xb[c].did_r(e2_b0b)
                e2_p2 = CH("scalar", [
                    lambda e, c=c: e.activation(lnv[:, c:c + 1], mvar[:, c, 1:2], AF.Ln, bias=lneps[:]),
                    lambda e, c=c: e.activation(rstd[:, c:c + 1], lnv[:, c:c + 1], AF.Exp, scale=-0.5),
                ], waits=[e2_b1, e_vones])
                e_h2a = CH("vector", [
                    lambda e, c=c: e.tensor_scalar(h2t[:], xb[:, c, :], mvar[:, c, 0:1],
                                                   rstd[:, c:c + 1], ALU.subtract, ALU.mult),
                    lambda e: e.tensor_tensor(h2t[:], h2t[:], cb1_rep[:], ALU.mult),
                ], waits=[e2_p2, e_rep1v] + g_h2.wr())
                g_h2.did_w(e_h2a)
                g_xb[c].did_r(e_h2a)

                def s2_h2b(e):
                    return e.tensor_tensor(h2t[:], h2t[:], sb_rep[:], ALU.add)
                e_h2 = EM("gpsimd", s2_h2b, waits=[e_h2a, e_rep2v])
                g_h2.w = e_h2

                if STAGE < 9:
                    continue
                e_xq_l = quant(h2t[:], xd[:], None, [e_h2], g_p5, g_src=g_h2)

                if STAGE < 10:
                    continue
                def s_xdt(e):
                    r = None
                    for half, bank in ((0, ps_a), (1, ps_c)):
                        for j in range(4):
                            kc = half * 4 + j
                            r = e.matmul(bank[:, j * 128:(j + 1) * 128],
                                         xd[:, kc * 128:(kc + 1) * 128], idb[:],
                                         start=True, stop=True)
                    return r
                e_xdt = EM("tensor", s_xdt, waits=e_xq_l + gA.wr() + gC.wr())
                gA.did_w(e_xdt); gC.did_w(e_xdt)
                g_p5.did_r(e_xdt)

                def s_xdtev(e):
                    e.tensor_copy(xdT[:, 0:4, :], ps_a[:, 0:512].rearrange("p (k x) -> p k x", k=4))
                    return e.tensor_copy(xdT[:, 4:8, :], ps_c[:, 0:512].rearrange("p (k x) -> p k x", k=4))
                e_xdte = EM("vector", s_xdtev, waits=[e_xdt] + g_p5.wr())
                g_p5.did_w(e_xdte)
                gA.did_r(e_xdte); gC.did_r(e_xdte)

                e_ffn = {}
                for half, bank, gbank in ((0, ps_d, gD), (1, ps_e, gE)):
                    def s_ffn(half=half, bank=bank):
                        def f(e):
                            for kc in range(8):
                                e.matmul(bank[:, :], xdT[:, kc, :],
                                         wdt_sb[:, kc, half * 512:(half + 1) * 512],
                                         start=(kc == 0), stop=False)
                            return e.matmul(bank[:, :], ones1b[:],
                                            ffnbrow[0:1, half * 512:(half + 1) * 512],
                                            start=False, stop=True)
                        return f
                    wts = [e_xdte, e_ffnbrow, e_wd_done] + gbank.wr()
                    e_f = EM("tensor", s_ffn(), waits=wts)
                    gbank.did_w(e_f)
                    g_p5.did_r(e_f)
                    e_ffn[half] = e_f

                e_st_l = []
                for half, bank, gbank in ((0, ps_d, gD), (1, ps_e, gE)):
                    def s_fep(c=c, half=half, bank=bank):
                        def f(e):
                            return e.tensor_tensor(
                                xb[:, c, half * 512:(half + 1) * 512],
                                xb[:, c, half * 512:(half + 1) * 512],
                                bank[:, :], ALU.add)
                        return f
                    e_fepi = EM("vector", s_fep(), waits=[e_ffn[half]])
                    gbank.did_r(e_fepi)
                    g_xb[c].did_r(e_fepi)
                    g_xb[c].w = e_fepi

                    def s_store(c=c, g=g, half=half):
                        def f(e):
                            return e.dma_start(
                                out=y_h[g * 128:(g + 1) * 128, half * 512:(half + 1) * 512],
                                in_=xb[:, c, half * 512:(half + 1) * 512])
                        return f
                    e_st = EM("scalar", s_store(), waits=[e_fepi], dma=True,
                              dsem=("dst0", "dst1", "dst2", "dst3")[c])
                    g_xb[c].did_r(e_st)
                    e_st_l.append(e_st)
                e_store[(t, c)] = e_st_l[1]

        final_dst = {s: pr.counts.get(s, 0) for s in ("dst0", "dst1", "dst2", "dst3")}

        with nc.Block() as block:
            for engname in ("sync", "vector", "scalar", "tensor", "gpsimd"):
                def engfn(eng, engname=engname):
                    for (en, fn, ws, sem, inc) in pr.steps:
                        if en != engname:
                            continue
                        for (wsem, wval) in ws:
                            eng.wait_ge(sems[wsem], wval)
                        inst = fn(eng)
                        inst.then_inc(sems[sem], inc)
                    if engname == "scalar":
                        for s_, v_ in final_dst.items():
                            if v_:
                                eng.wait_ge(sems[s_], v_)
                getattr(block, engname)(engfn)

    return nc


_NC_CACHE = [None]


def kernel(L0_attn_x, L0_attn_Wq, L0_attn_bq, L0_attn_Wk, L0_attn_bk,
           L0_attn_Wv, L0_attn_bv, L0_attn_Wo, L0_attn_bo, L0_attn_proj,
           L0_ffn_weight, L0_ffn_bias, L0_c, L0_wmod):
    x = np.asarray(L0_attn_x)
    assert x.shape == (8, T, D)

    def colsof(b):
        return np.ascontiguousarray(np.asarray(b, np.float32).reshape(8, 128).T)

    common = dict(
        wqt=np.ascontiguousarray(np.asarray(L0_attn_Wq, np.float32).T),
        wkt=np.ascontiguousarray(np.asarray(L0_attn_Wk, np.float32).T),
        wvt=np.ascontiguousarray(np.asarray(L0_attn_Wv, np.float32).T),
        wo=np.ascontiguousarray(np.asarray(L0_attn_Wo, np.float32)),
        wffn=np.ascontiguousarray(np.asarray(L0_ffn_weight, np.float32)),
        wmod=np.ascontiguousarray(np.asarray(L0_wmod, np.float32)),
        projt=np.ascontiguousarray(np.asarray(L0_attn_proj, np.float32).T),
        bqc=colsof(L0_attn_bq), bkc=colsof(L0_attn_bk), bvc=colsof(L0_attn_bv),
        boc=colsof(L0_attn_bo), bffc=colsof(L0_ffn_bias),
        idb=np.eye(128, dtype=ml_dtypes.bfloat16),
        idf8=np.eye(8, dtype=np.float32),
        maskt=np.triu(np.ones((128, 128), dtype=np.float32)),
        ones1b=np.ones((1, 128), dtype=ml_dtypes.bfloat16),
        ones1f=np.ones((1, 128), dtype=np.float32),
    )
    negblk = np.zeros((128, 8, 16), dtype=np.float32)
    for oc in range(8):
        negblk[0:64, oc, 2 * oc] = -1.0 / 16.0
        negblk[64:128, oc, 2 * oc + 1] = -1.0 / 16.0
    common["negblk"] = negblk.astype(ml_dtypes.bfloat16)

    cmat = np.asarray(L0_c, np.float32)
    in_maps = []
    for b in range(8):
        m = dict(common)
        m["x"] = np.ascontiguousarray(np.asarray(x[b], np.float32))
        m["crow"] = np.ascontiguousarray(cmat[b].reshape(8, 128))
        in_maps.append(m)

    nc = build_program()
    res = run_bass_kernel_spmd(nc, in_maps, core_ids=list(range(8)))
    out = np.stack([np.asarray(r["y"], np.float32) for r in res.results], axis=0)
    return out
